# version 09
# baseline (speedup 1.0000x reference)
import numpy as np
from contextlib import ExitStack

import concourse.bass as bass
import concourse.mybir as mybir
from concourse.bass_utils import run_bass_kernel_spmd

F32 = mybir.dt.float32
BF16 = mybir.dt.bfloat16
AF = mybir.ActivationFunctionType
ALU = mybir.AluOpType

P = 128
T = 4096
EH = 8
H = 1024
NST = 16
J = 1056
FH = 4096
CH = 512
NCHUNK = T // CH
TS = 32
NSUB = CH // TS
TT = 512
NTILE = T // TT
G = 128
EPS = 1e-5


class Sems:

    def __init__(self, nc, stack, names):
        self.h = {n: stack.enter_context(nc.semaphore(n)) for n in names}
        self.cnt = {n: 0 for n in names}
        self.waited = {}

    def inc(self, name, instr, k=1):
        instr.then_inc(self.h[name], k)
        self.cnt[name] += k
        return self.cnt[name]

    def val(self, name):
        return self.cnt[name]

    def wait(self, eng, ename, name, target):
        if target is None or target <= 0:
            return
        key = (ename, name)
        if self.waited.get(key, 0) >= target:
            return
        eng.wait_ge(self.h[name], target)
        self.waited[key] = target


def build_program(phases=4, dbg=False):
    SCR = "ExternalOutput" if dbg else "Internal"
    nc = bass.Bass("TRN2", target_bir_lowering=False, debug=False)

    xin = nc.dram_tensor("xin", [H, T], F32, kind="ExternalInput").ap()
    wxt = nc.dram_tensor("wxt", [H, J], F32, kind="ExternalInput").ap()
    w1 = nc.dram_tensor("w1", [H, FH], F32, kind="ExternalInput").ap()
    w2 = nc.dram_tensor("w2", [FH, H], F32, kind="ExternalInput").ap()
    wot = nc.dram_tensor("wot", [H, H], F32, kind="ExternalInput").ap()
    alog = nc.dram_tensor("alog", [1, NST], F32, kind="ExternalInput").ap()
    dvec = nc.dram_tensor("dvec", [P, EH], F32, kind="ExternalInput").ap()
    b1m = nc.dram_tensor("b1m", [P, 32], F32, kind="ExternalInput").ap()
    b2m = nc.dram_tensor("b2m", [P, EH], F32, kind="ExternalInput").ap()
    yout = nc.dram_tensor("yout", [H, T], F32, kind="ExternalOutput").ap()
    a_d = nc.dram_tensor("a_d", [1, NST], F32, kind="Internal").ap()
    ab_d = nc.dram_tensor("ab_d", [2, T], F32, kind=SCR).ap()
    bc_d = nc.dram_tensor("bc_d", [32, T], F32, kind=SCR).ap()
    xn_d = nc.dram_tensor("xn_d", [H, T], BF16, kind=SCR).ap()
    delta_d = nc.dram_tensor("delta_d", [H, T], BF16, kind=SCR).ap()
    attn_d = nc.dram_tensor("attn_d", [H, T], BF16, kind=SCR).ap()
    acca_d = nc.dram_tensor("acca_d", [H, T], BF16, kind=SCR).ap()

    xin3 = xin.rearrange("(a p) t -> p a t", p=P)
    xn3 = xn_d.rearrange("(a p) t -> p a t", p=P)
    delta3 = delta_d.rearrange("(a p) t -> p a t", p=P)
    attn3 = attn_d.rearrange("(a p) t -> p a t", p=P)
    acca3 = acca_d.rearrange("(a p) t -> p a t", p=P)
    yout3 = yout.rearrange("(a p) t -> p a t", p=P)

    with ExitStack() as outer:
        sem = Sems(nc, outer, [
            "lds",
            "sts",
            "ldg",
            "mm",
            "ev",
            "vv",
            "bs", "bg", "bt", "bc_", "bv",
        ])

        def sb(stack, name, shape, dt):
            return stack.enter_context(nc.sbuf_tensor(name, shape, dt))

        A_bc = sb(outer, "A_bc", [P, NST], F32)
        D_sb = sb(outer, "D_sb", [P, EH], F32)
        B1_sb = sb(outer, "B1_sb", [P, 32], F32)
        B2_sb = sb(outer, "B2_sb", [P, EH], F32)
        ones = sb(outer, "ones", [P, 1], F32)
        arow = sb(outer, "arow", [1, NST], F32)

        ps = [outer.enter_context(nc.psum_tensor(f"ps{i}", [P, 512], F32))
              for i in range(6)]
        bank_free = {i: None for i in range(6)}

        block = outer.enter_context(nc.Block())
        barrier_n = [0]

        def barrier():
            barrier_n[0] += 1
            k = barrier_n[0]
            pairs = [("sync", "bs"), ("gpsimd", "bg"), ("tensor", "bt"),
                     ("scalar", "bc_"), ("vector", "bv")]
            for ename, s in pairs:
                @getattr(block, ename)
                def _(eng, s=s):
                    eng.sem_inc(sem.h[s], 1)
                sem.cnt[s] += 1
            for ename, _s in pairs:
                @getattr(block, ename)
                def _(eng, ename=ename):
                    for _, s2 in pairs:
                        sem.wait(eng, ename, s2, k)

        with ExitStack() as s13:
            Wot = sb(s13, "Wot", [P, EH * H], BF16)

            @block.sync
            def _(sync):
                sync.dma_start(out=arow[:, :], in_=alog[:, :]).then_inc(sem.h["lds"], 16)
                sem.cnt["lds"] += 16
                sync.dma_start(out=D_sb[:, :], in_=dvec[:, :]).then_inc(sem.h["lds"], 16)
                sem.cnt["lds"] += 16
                sync.dma_start(out=B1_sb[:, :], in_=b1m[:, :]).then_inc(sem.h["lds"], 16)
                sem.cnt["lds"] += 16
                sync.dma_start(out=B2_sb[:, :], in_=b2m[:, :]).then_inc(sem.h["lds"], 16)
                sem.cnt["lds"] += 16
            ph0_lds = sem.val("lds")

            @block.gpsimd
            def _(gp):
                i = gp.dma_start(out=Wot[:, :].rearrange("p (a e) -> p a e", a=EH),
                                 in_=wot.rearrange("(a p) e -> p a e", p=P))
                sem.inc("ldg", i, 16)
            wx_ld = sem.val("ldg")

            @block.scalar
            def _(sc):
                sem.wait(sc, "scalar", "lds", ph0_lds)
                i = sc.activation(out=arow[:, :], in_=arow[:, :], func=AF.Exp)
                sem.inc("ev", i)
            a_exp = sem.val("ev")

            @block.vector
            def _(v):
                v.memset(ones[:, :], 1.0)
                sem.wait(v, "vector", "ev", a_exp)
                i = v.tensor_scalar(out=arow[:, :], in0=arow[:, :], scalar1=-1.0,
                                    scalar2=None, op0=ALU.mult)
                sem.inc("vv", i)
            a_neg = sem.val("vv")

            @block.sync
            def _(sync):
                sem.wait(sync, "sync", "vv", a_neg)
                i = sync.dma_start(out=a_d[:, :], in_=arow[:, :])
                sem.inc("sts", i, 16)
            a_spill = sem.val("sts")

            @block.gpsimd
            def _(gp):
                sem.wait(gp, "gpsimd", "sts", a_spill)
                i = gp.dma_start(out=A_bc[:, :], in_=a_d[0:1, :].broadcast_to([P, NST]))
                sem.inc("ldg", i, 16)
            abc_ld = sem.val("ldg")

            with ExitStack() as s1:
                WxT = sb(s1, "WxT", [P, EH * J], BF16)
                xt = [sb(s1, f"xt{b}", [P, EH * TT], F32) for b in range(2)]
                xsq = sb(s1, "xsq", [P, EH * TT], F32)
                xn_t = [sb(s1, f"xnt{b}", [P, EH * TT], BF16) for b in range(2)]
                dl_t = [sb(s1, f"dlt{b}", [P, EH * TT], BF16) for b in range(2)]
                BCs = sb(s1, "BCs", [32, TT], F32)
                a_bc = sb(s1, "a_bc", [P, TT], F32)
                b_bc = sb(s1, "b_bc", [P, TT], F32)
                rows = sb(s1, "rows", [1, 512 * 10], F32)

                def row(i):
                    return rows[:, i * 512:(i + 1) * 512]

                @block.gpsimd
                def _(gp):
                    i = gp.dma_start(out=WxT[:, :].rearrange("p (a j) -> p a j", a=EH),
                                     in_=wxt.rearrange("(a p) j -> p a j", p=P))
                    sem.inc("ldg", i, 16)
                wxt_ld = sem.val("ldg")

                x_loaded = []
                xt_free = [None, None]
                xn_free = [None, None]
                xsq_free = None
                abc_free = None
                ab_sp_prev = None
                bc_sp_prev = None
                bc_sp = []
                dl_free = [None, None]
                spill_cnt = []
                dl_spill = []
                dlb_evcnt = {}

                spsc = sb(s1, "spsc", [P, TT], F32)
                xn_done_t = {}

                def stage_a(tt):
                    nonlocal abc_free, ab_sp_prev
                    b = tt % 2
                    t0 = tt * TT

                    @block.sync
                    def _(sync, b=b, t0=t0):
                        if xt_free[b] is not None:
                            sem.wait(sync, "sync", *xt_free[b])
                        i = sync.dma_start(out=xt[b][:, :].rearrange("p (a t) -> p a t", a=EH),
                                           in_=xin3[:, :, t0:t0 + TT])
                        sem.inc("lds", i, 16)
                    x_loaded.append(sem.val("lds"))

                    @block.scalar
                    def _(sc, b=b):
                        sem.wait(sc, "scalar", "lds", x_loaded[-1])
                        if xsq_free[0] is not None:
                            sem.wait(sc, "scalar", *xsq_free[0])
                        i = sc.activation(out=xsq[:, :], in_=xt[b][:, :], func=AF.Square)
                        sem.inc("ev", i)
                    xsq_done = sem.val("ev")

                    @block.tensor
                    def _(te, b=b):
                        sem.wait(te, "tensor", "lds", x_loaded[-1])
                        sem.wait(te, "tensor", "ev", xsq_done)
                        if bank_free[4] is not None:
                            sem.wait(te, "tensor", *bank_free[4])
                        if bank_free[5] is not None:
                            sem.wait(te, "tensor", *bank_free[5])
                        for kc in range(EH):
                            te.matmul(out=ps[4][0:1, 0:TT], lhsT=ones[:, :],
                                      rhs=xt[b][:, kc * TT:(kc + 1) * TT],
                                      start=(kc == 0), stop=(kc == EH - 1))
                        for kc in range(EH):
                            i = te.matmul(out=ps[5][0:1, 0:TT], lhsT=ones[:, :],
                                          rhs=xsq[:, kc * TT:(kc + 1) * TT],
                                          start=(kc == 0), stop=(kc == EH - 1))
                        sem.inc("mm", i)
                    stats_done = sem.val("mm")
                    xsq_free[0] = ("mm", stats_done)

                    @block.vector
                    def _(v):
                        sem.wait(v, "vector", "mm", stats_done)
                        if ab_sp_prev is not None:
                            sem.wait(v, "vector", "sts", ab_sp_prev)
                        v.tensor_scalar(out=row(0), in0=ps[4][0:1, 0:TT],
                                        scalar1=1.0 / H, scalar2=None, op0=ALU.mult)
                        v.tensor_scalar(out=row(1), in0=ps[5][0:1, 0:TT],
                                        scalar1=1.0 / H, scalar2=None, op0=ALU.mult)
                        v.tensor_mul(row(2), row(0), row(0))
                        v.tensor_sub(row(3), row(1), row(2))
                        i = v.tensor_scalar(out=row(4), in0=row(3), scalar1=EPS,
                                            scalar2=None, op0=ALU.add)
                        sem.inc("vv", i)
                    veps_done = sem.val("vv")
                    bank_free[4] = ("vv", veps_done)
                    bank_free[5] = ("vv", veps_done)

                    @block.scalar
                    def _(sc):
                        sem.wait(sc, "scalar", "vv", veps_done)
                        i = sc.activation(out=row(5), in_=row(4), func=AF.Sqrt)
                        sem.inc("ev", i)
                    sq_done = sem.val("ev")

                    @block.vector
                    def _(v):
                        sem.wait(v, "vector", "ev", sq_done)
                        v.reciprocal(row(6), row(5))
                        v.tensor_mul(row(2), row(6), row(6))
                        v.tensor_mul(row(1), row(2), row(4))
                        v.tensor_scalar(out=row(2), in0=row(1), scalar1=-0.5,
                                        scalar2=1.5, op0=ALU.mult, op1=ALU.add)
                        v.tensor_mul(row(7), row(6), row(2))
                        v.tensor_mul(row(8), row(0), row(7))
                        i = v.tensor_scalar(out=row(8), in0=row(8), scalar1=-1.0,
                                            scalar2=None, op0=ALU.mult)
                        sem.inc("vv", i)
                    ab_done = sem.val("vv")

                    @block.sync
                    def _(sync, t0=t0):
                        sem.wait(sync, "sync", "vv", ab_done)
                        sync.dma_start(out=ab_d[0:1, t0:t0 + TT], in_=row(7)).then_inc(sem.h["sts"], 16)
                        sem.cnt["sts"] += 16
                        i = sync.dma_start(out=ab_d[1:2, t0:t0 + TT], in_=row(8))
                        sem.inc("sts", i, 16)
                    ab_sp = sem.val("sts")
                    ab_sp_prev = ab_sp

                    @block.gpsimd
                    def _(gp, t0=t0):
                        sem.wait(gp, "gpsimd", "sts", ab_sp)
                        if abc_free is not None:
                            sem.wait(gp, "gpsimd", *abc_free)
                        gp.dma_start(out=a_bc[:, :],
                                     in_=ab_d[0:1, t0:t0 + TT].broadcast_to([P, TT])).then_inc(sem.h["ldg"], 16)
                        sem.cnt["ldg"] += 16
                        i = gp.dma_start(out=b_bc[:, :],
                                         in_=ab_d[1:2, t0:t0 + TT].broadcast_to([P, TT]))
                        sem.inc("ldg", i, 16)
                    abc_ld2 = sem.val("ldg")

                    @block.vector
                    def _(v, b=b):
                        sem.wait(v, "vector", "ldg", abc_ld2)
                        if xn_free[b] is not None:
                            for cond in xn_free[b]:
                                sem.wait(v, "vector", *cond)
                        v.tensor_tensor(
                            out=xt[b][:, :].rearrange("p (a t) -> p a t", a=EH),
                            in0=xt[b][:, :].rearrange("p (a t) -> p a t", a=EH),
                            in1=a_bc[:, :].unsqueeze(1).broadcast_to([P, EH, TT]),
                            op=ALU.mult)
                        i = v.tensor_tensor(
                            out=xn_t[b][:, :].rearrange("p (a t) -> p a t", a=EH),
                            in0=xt[b][:, :].rearrange("p (a t) -> p a t", a=EH),
                            in1=b_bc[:, :].unsqueeze(1).broadcast_to([P, EH, TT]),
                            op=ALU.add)
                        sem.inc("vv", i)
                    xn_done_t[tt] = sem.val("vv")
                    xt_free[b] = ("vv", xn_done_t[tt])
                    abc_free = ("vv", xn_done_t[tt])

                    @block.sync
                    def _(sync, b=b, t0=t0, tt=tt):
                        sem.wait(sync, "sync", "vv", xn_done_t[tt])
                        i = sync.dma_start(out=xn3[:, :, t0:t0 + TT],
                                           in_=xn_t[b][:, :].rearrange("p (a t) -> p a t", a=EH))
                        sem.inc("sts", i, 16)
                    spill_cnt.append(sem.val("sts"))

                def stage_b(tt):
                    nonlocal bc_sp_prev
                    b = tt % 2
                    t0 = tt * TT
                    ev_base2 = sem.val("ev")
                    for jb in [8] + list(range(8)):
                        bank = 2 if jb == 8 else jb % 2
                        mwid = 32 if jb == 8 else P

                        @block.tensor
                        def _(te, b=b, tt=tt, jb=jb, bank=bank, mwid=mwid):
                            sem.wait(te, "tensor", "vv", xn_done_t[tt])
                            sem.wait(te, "tensor", "ldg", wxt_ld)
                            if jb in (8, 0, 1):
                                if bank_free[bank] is not None:
                                    sem.wait(te, "tensor", *bank_free[bank])
                            else:
                                sem.wait(te, "tensor", "ev", ev_base2 + jb)
                            for kc in range(EH):
                                i = te.matmul(
                                    out=ps[bank][0:mwid, 0:TT],
                                    lhsT=WxT[:, kc * J + jb * P:kc * J + jb * P + mwid],
                                    rhs=xn_t[b][:, kc * TT:(kc + 1) * TT],
                                    start=(kc == 0), stop=(kc == EH - 1))
                            sem.inc("mm", i)
                            dlb_evcnt[(tt, jb)] = sem.val("mm")
                    wx_done = sem.val("mm")
                    xn_free[b] = [("sts", spill_cnt[tt]), ("mm", wx_done)]

                    @block.scalar
                    def _(sc, b=b, tt=tt):
                        sem.wait(sc, "scalar", "mm", dlb_evcnt[(tt, 8)])
                        if bc_sp_prev is not None:
                            sem.wait(sc, "scalar", "sts", bc_sp_prev)
                        i = sc.activation(out=BCs[:, :], in_=ps[2][0:32, 0:TT],
                                          func=AF.Copy)
                        sem.inc("ev", i)
                        if dl_free[b] is not None:
                            sem.wait(sc, "scalar", *dl_free[b])
                        for jb in range(8):
                            bank = jb % 2
                            sem.wait(sc, "scalar", "mm", dlb_evcnt[(tt, jb)])
                            sc.activation(out=spsc[:, :], in_=ps[bank][:, 0:TT],
                                          func=AF.Exp)
                            i = sc.activation(out=dl_t[b][:, jb * TT:(jb + 1) * TT],
                                              in_=spsc[:, :], func=AF.Ln, bias=1.0)
                            sem.inc("ev", i)
                    assert sem.val("ev") == ev_base2 + 9
                    bank_free[2] = ("ev", ev_base2 + 1)
                    bank_free[0] = ("ev", ev_base2 + 8)
                    bank_free[1] = ("ev", ev_base2 + 9)
                    soft_done = sem.val("ev")

                    @block.sync
                    def _(sync, b=b, t0=t0, tt=tt):
                        sem.wait(sync, "sync", "ev", soft_done)
                        sync.dma_start(out=bc_d[:, t0:t0 + TT],
                                       in_=BCs[:, :]).then_inc(sem.h["sts"], 16)
                        sem.cnt["sts"] += 16
                        i = sync.dma_start(out=delta3[:, :, t0:t0 + TT],
                                           in_=dl_t[b][:, :].rearrange("p (a t) -> p a t", a=EH))
                        sem.inc("sts", i, 16)
                    dl_spill.append(sem.val("sts"))
                    bc_sp.append(sem.val("sts"))
                    bc_sp_prev = sem.val("sts")
                    dl_free[b] = ("sts", dl_spill[-1])

                xsq_free = [None]
                stage_a(0)
                stage_a(1)
                for tt in range(NTILE):
                    stage_b(tt)
                    if tt + 2 < NTILE:
                        stage_a(tt + 2)

            barrier()

            if phases < 3:
                @block.sync
                def _(sync):
                    sem.wait(sync, "sync", "sts", sem.val("sts"))
                return nc
            with ExitStack() as s3:
                HC = CH // 2
                xnb = sb(s3, "xnb", [P, EH * HC], BF16)
                dl_c = sb(s3, "dlc", [P, EH * CH], BF16)
                dtx = sb(s3, "dtx", [P, EH * HC], BF16)
                dA = [sb(s3, f"dA{b}", [P, G * TS], BF16) for b in range(2)]
                ub = sb(s3, "ub", [P, G * TS], BF16)
                hb = sb(s3, "hb", [P, G * TS], BF16)
                l4f = sb(s3, "l4f", [P, EH * TS], F32)
                yb = sb(s3, "yb", [P, EH * CH], BF16)
                att_t = sb(s3, "att_t", [P, EH * CH], BF16)
                carry = sb(s3, "carry", [P, G], F32)
                dA0 = sb(s3, "dA0v", [P, G], F32)
                tmpg = sb(s3, "tmpg", [P, G], F32)
                Bball = sb(s3, "Bball", [P, NST * HC], BF16)
                Cball = sb(s3, "Cball", [P, NST * HC], BF16)
                HHALF = FH // 2
                FT = 256
                NFT = T // FT
                xnf = sb(s3, "xnf", [P, EH * FT], BF16)
                w1h = sb(s3, "w1h", [P, EH * HHALF], BF16)
                w2h = sb(s3, "w2h", [P, 16 * H], BF16)
                resf = sb(s3, "resf", [P, EH * FT], F32)
                attf = sb(s3, "attf", [P, EH * FT], BF16)
                acb = sb(s3, "acb", [P, EH * FT], BF16)
                hf = sb(s3, "hf", [P, 16 * FT], BF16)
                outf = sb(s3, "outf", [P, EH * FT], F32)

                @block.vector
                def _(v):
                    v.memset(carry[:, :], 0.0)

                @block.gpsimd
                def _(gp):
                    gp.dma_start(
                        out=w1h[:, :].rearrange("p (a h) -> p a h", a=EH),
                        in_=w1.rearrange("(a p) h -> p a h", p=P)[:, :, 0:HHALF]
                    ).then_inc(sem.h["ldg"], 16)
                    sem.cnt["ldg"] += 16
                    i = gp.dma_start(
                        out=w2h[:, :].rearrange("p (a e) -> p a e", a=16),
                        in_=w2.rearrange("(a p) e -> p a e", p=P)[:, 0:16, :])
                    sem.inc("ldg", i, 16)
                w_ld = sem.val("ldg")

                sub_done = []
                y_done = []
                att_sp = []
                acca_sp = {}
                dlb_evcnt = {}
                st = {"xnb_free": None, "dl_free": None, "yb_free": None,
                      "att_free": None, "resf_free": None, "attf_free": None,
                      "outf_free": None, "hf_free": None, "bc_free": None,
                      "passA_mm": None, "dtx_done": 0, "xnf_free": None,
                      "wb_ld": None, "acb_free": None}
                pend_fin = []

                def emit_wout(cc):
                    evw_base = sem.val("ev")
                    for eb in range(EH):
                        bank = 2 + eb % 2

                        @block.tensor
                        def _(te, cc=cc, eb=eb, bank=bank):
                            sem.wait(te, "tensor", "vv", y_done[cc])
                            sem.wait(te, "tensor", "ldg", wx_ld)
                            if eb < 2:
                                if bank_free[bank] is not None:
                                    sem.wait(te, "tensor", *bank_free[bank])
                            else:
                                sem.wait(te, "tensor", "ev", evw_base + eb - 1)
                            for kc in range(EH):
                                i = te.matmul(
                                    out=ps[bank][:, 0:CH],
                                    lhsT=Wot[:, kc * H + eb * P:kc * H + (eb + 1) * P],
                                    rhs=yb[:, kc * CH:(kc + 1) * CH],
                                    start=(kc == 0), stop=(kc == EH - 1))
                            sem.inc("mm", i)
                            dlb_evcnt[("wo", cc, eb)] = sem.val("mm")

                    @block.scalar
                    def _(sc, cc=cc):
                        if att_sp:
                            sem.wait(sc, "scalar", "sts", att_sp[-1])
                        for eb in range(EH):
                            bank = 2 + eb % 2
                            sem.wait(sc, "scalar", "mm", dlb_evcnt[("wo", cc, eb)])
                            i = sc.activation(out=att_t[:, eb * CH:(eb + 1) * CH],
                                              in_=ps[bank][:, 0:CH], func=AF.Copy)
                            sem.inc("ev", i)
                    assert sem.val("ev") == evw_base + EH
                    bank_free[2] = ("ev", evw_base + EH - 1)
                    bank_free[3] = ("ev", evw_base + EH)
                    wev = sem.val("ev")

                    @block.sync
                    def _(sync, cc=cc, wev=wev):
                        sem.wait(sync, "sync", "ev", wev)
                        i = sync.dma_start(out=attn3[:, :, cc * CH:(cc + 1) * CH],
                                           in_=att_t[:, :].rearrange("p (a t) -> p a t", a=EH))
                        sem.inc("sts", i, 16)
                    att_sp.append(sem.val("sts"))
                    st["yb_free"] = ("mm", dlb_evcnt[("wo", cc, EH - 1)])

                def emit_ffn_tile(pa, tf):
                    t0 = tf * FT

                    @block.sync
                    def _(sync, pa=pa, tf=tf, t0=t0):
                        if st["xnf_free"] is not None:
                            sem.wait(sync, "sync", *st["xnf_free"])
                        sem.wait(sync, "sync", "sts", spill_cnt[tf // 2])
                        i = sync.dma_start(out=xnf[:, :].rearrange("p (a t) -> p a t", a=EH),
                                           in_=xn3[:, :, t0:t0 + FT])
                        sem.inc("lds", i, 16)
                        if pa == 1:
                            if st["resf_free"] is not None:
                                sem.wait(sync, "sync", *st["resf_free"])
                            sync.dma_start(out=resf[:, :].rearrange("p (a t) -> p a t", a=EH),
                                           in_=xin3[:, :, t0:t0 + FT]).then_inc(sem.h["lds"], 16)
                            sem.cnt["lds"] += 16
                            if st["acb_free"] is not None:
                                sem.wait(sync, "sync", *st["acb_free"])
                            sem.wait(sync, "sync", "sts", acca_sp[tf])
                            sync.dma_start(out=acb[:, :].rearrange("p (a t) -> p a t", a=EH),
                                           in_=acca3[:, :, t0:t0 + FT]).then_inc(sem.h["lds"], 16)
                            sem.cnt["lds"] += 16
                            if st["attf_free"] is not None:
                                sem.wait(sync, "sync", *st["attf_free"])
                            sem.wait(sync, "sync", "sts", att_sp[tf // 2])
                            i = sync.dma_start(out=attf[:, :].rearrange("p (a t) -> p a t", a=EH),
                                               in_=attn3[:, :, t0:t0 + FT])
                            sem.inc("lds", i, 16)
                    f_ld = sem.val("lds")

                    evt_base = sem.val("ev")
                    for hb_ in range(16):
                        bank = hb_ % 2

                        @block.tensor
                        def _(te, pa=pa, hb_=hb_, bank=bank):
                            sem.wait(te, "tensor", "lds", f_ld)
                            sem.wait(te, "tensor", "ldg",
                                     w_ld if pa == 0 else st["wb_ld"])
                            if hb_ == 0 and st["hf_free"] is not None:
                                sem.wait(te, "tensor", *st["hf_free"])
                            if hb_ < 2:
                                if bank_free[bank] is not None:
                                    sem.wait(te, "tensor", *bank_free[bank])
                            else:
                                sem.wait(te, "tensor", "ev", evt_base + hb_ - 1)
                            for kc in range(EH):
                                i = te.matmul(
                                    out=ps[bank][:, 0:FT],
                                    lhsT=w1h[:, kc * HHALF + hb_ * P:kc * HHALF + (hb_ + 1) * P],
                                    rhs=xnf[:, kc * FT:(kc + 1) * FT],
                                    start=(kc == 0), stop=(kc == EH - 1))
                            sem.inc("mm", i)
                            dlb_evcnt[("f1", pa, tf, hb_)] = sem.val("mm")

                    @block.scalar
                    def _(sc, pa=pa, tf=tf):
                        for hb_ in range(16):
                            bank = hb_ % 2
                            sem.wait(sc, "scalar", "mm", dlb_evcnt[("f1", pa, tf, hb_)])
                            i = sc.activation(out=hf[:, hb_ * FT:(hb_ + 1) * FT],
                                              in_=ps[bank][:, 0:FT], func=AF.Gelu,
                                              bias=B1_sb[:, pa * 16 + hb_:pa * 16 + hb_ + 1])
                            sem.inc("ev", i)
                    assert sem.val("ev") == evt_base + 16
                    bank_free[0] = ("ev", evt_base + 15)
                    bank_free[1] = ("ev", evt_base + 16)
                    gelu_done = sem.val("ev")

                    evq_base = sem.val("ev")
                    for eb in range(EH):
                        bank = 4 + eb % 2

                        @block.tensor
                        def _(te, pa=pa, eb=eb, bank=bank):
                            sem.wait(te, "tensor", "ev", gelu_done)
                            if eb < 2:
                                if bank_free[bank] is not None:
                                    sem.wait(te, "tensor", *bank_free[bank])
                            else:
                                sem.wait(te, "tensor", "ev", evq_base + eb - 1)
                            for kc in range(16):
                                i = te.matmul(
                                    out=ps[bank][:, 0:FT],
                                    lhsT=w2h[:, kc * H + eb * P:kc * H + (eb + 1) * P],
                                    rhs=hf[:, kc * FT:(kc + 1) * FT],
                                    start=(kc == 0), stop=(kc == 15))
                            sem.inc("mm", i)
                            dlb_evcnt[("f2", pa, tf, eb)] = sem.val("mm")
                    st["hf_free"] = ("mm", dlb_evcnt[("f2", pa, tf, EH - 1)])
                    st["xnf_free"] = ("mm", dlb_evcnt[("f1", pa, tf, 15)])

                    @block.scalar
                    def _(sc, pa=pa, tf=tf):
                        if pa == 0 and st["attf_free"] is not None:
                            sem.wait(sc, "scalar", *st["attf_free"])
                        if pa == 1 and st["outf_free"] is not None:
                            sem.wait(sc, "scalar", *st["outf_free"])
                        for eb in range(EH):
                            bank = 4 + eb % 2
                            sem.wait(sc, "scalar", "mm", dlb_evcnt[("f2", pa, tf, eb)])
                            if pa == 0:
                                i = sc.activation(out=attf[:, eb * FT:(eb + 1) * FT],
                                                  in_=ps[bank][:, 0:FT], func=AF.Copy)
                            else:
                                i = sc.activation(out=outf[:, eb * FT:(eb + 1) * FT],
                                                  in_=ps[bank][:, 0:FT], func=AF.Identity,
                                                  bias=B2_sb[:, eb:eb + 1])
                            sem.inc("ev", i)
                    assert sem.val("ev") == evq_base + EH
                    bank_free[4] = ("ev", evq_base + EH - 1)
                    bank_free[5] = ("ev", evq_base + EH)
                    fev = sem.val("ev")

                    if pa == 0:
                        @block.sync
                        def _(sync, tf=tf, t0=t0, fev=fev):
                            sem.wait(sync, "sync", "ev", fev)
                            i = sync.dma_start(out=acca3[:, :, t0:t0 + FT],
                                               in_=attf[:, :].rearrange("p (a t) -> p a t", a=EH))
                            sem.inc("sts", i, 16)
                        acca_sp[tf] = sem.val("sts")
                        st["attf_free"] = ("sts", acca_sp[tf])
                        st["passA_mm"] = st["hf_free"]
                        pend_fin.append(None)
                    else:
                        pend_fin.append((tf, t0, fev, f_ld))

                def emit_ffn_fin():
                    while pend_fin:
                        item = pend_fin.pop(0)
                        if item is None:
                            continue
                        tf, t0, fev, f_ld = item

                        @block.vector
                        def _(v, fev=fev, f_ld=f_ld):
                            sem.wait(v, "vector", "ev", fev)
                            sem.wait(v, "vector", "lds", f_ld)
                            v.tensor_tensor(out=outf[:, :], in0=outf[:, :],
                                            in1=resf[:, :], op=ALU.add)
                            v.tensor_tensor(out=outf[:, :], in0=outf[:, :],
                                            in1=acb[:, :], op=ALU.add)
                            i = v.tensor_tensor(out=outf[:, :], in0=outf[:, :],
                                                in1=attf[:, :], op=ALU.add)
                            sem.inc("vv", i)
                        fin = sem.val("vv")
                        st["resf_free"] = ("vv", fin)
                        st["attf_free"] = ("vv", fin)
                        st["acb_free"] = ("vv", fin)

                        @block.sync
                        def _(sync, tf=tf, t0=t0, fin=fin):
                            sem.wait(sync, "sync", "vv", fin)
                            i = sync.dma_start(out=yout3[:, :, t0:t0 + FT],
                                               in_=outf[:, :].rearrange("p (a t) -> p a t", a=EH))
                            sem.inc("sts", i, 16)
                        st["outf_free"] = ("sts", sem.val("sts"))

                for cc in range(NCHUNK):
                    ct0 = cc * CH

                    @block.sync
                    def _(sync, cc=cc, ct0=ct0):
                        sem.wait(sync, "sync", "sts", dl_spill[cc])
                        if st["xnb_free"] is not None:
                            sem.wait(sync, "sync", *st["xnb_free"])
                        if cc >= 1:
                            sem.wait(sync, "sync", "vv", sub_done[cc * NSUB - 1])
                        sync.dma_start(out=xnb[:, :].rearrange("p (a t) -> p a t", a=EH),
                                       in_=xn3[:, :, ct0:ct0 + HC]).then_inc(sem.h["lds"], 16)
                        sem.cnt["lds"] += 16
                        i = sync.dma_start(out=dl_c[:, :].rearrange("p (a t) -> p a t", a=EH),
                                           in_=delta3[:, :, ct0:ct0 + CH])
                        sem.inc("lds", i, 16)
                    c_ld = sem.val("lds")

                    @block.gpsimd
                    def _(gp, cc=cc, ct0=ct0):
                        sem.wait(gp, "gpsimd", "sts", bc_sp[cc])
                        if cc >= 1:
                            sem.wait(gp, "gpsimd", "vv", sub_done[cc * NSUB - 1])
                        gp.dma_start(
                            out=Bball[:, :].rearrange("p (n t) -> p n t", n=NST),
                            in_=bc_d[0:NST, ct0:ct0 + HC].unsqueeze(0).broadcast_to([P, NST, HC])
                        ).then_inc(sem.h["ldg"], 16)
                        sem.cnt["ldg"] += 16
                        i = gp.dma_start(
                            out=Cball[:, :].rearrange("p (n t) -> p n t", n=NST),
                            in_=bc_d[NST:32, ct0:ct0 + HC].unsqueeze(0).broadcast_to([P, NST, HC]))
                        sem.inc("ldg", i, 16)
                    bc_ld = [sem.val("ldg"), None]

                    if cc >= 1:
                        emit_wout(cc - 1)

                    @block.vector
                    def _(v, cc=cc):
                        sem.wait(v, "vector", "lds", c_ld)
                        if st["yb_free"] is not None:
                            sem.wait(v, "vector", *st["yb_free"])
                        v.tensor_tensor(out=dtx[:, :],
                                        in0=dl_c[:, :].rearrange("p (a t) -> p a t", a=EH)[:, :, 0:HC],
                                        in1=xnb[:, :].rearrange("p (a t) -> p a t", a=EH),
                                        op=ALU.mult)
                        for chx in range(EH):
                            i = v.tensor_scalar(out=yb[:, chx * CH:chx * CH + HC],
                                                in0=xnb[:, chx * HC:(chx + 1) * HC],
                                                scalar1=D_sb[:, chx:chx + 1],
                                                scalar2=None, op0=ALU.mult)
                        sem.inc("vv", i)
                    st["dtx_done"] = sem.val("vv")
                    st["xnb_free"] = ("vv", st["dtx_done"])

                    if cc <= 3:
                        for q in range(4):
                            emit_ffn_tile(0, 4 * cc + q)
                        if cc == 3:
                            @block.gpsimd
                            def _(gp):
                                sem.wait(gp, "gpsimd", *st["passA_mm"])
                                gp.dma_start(
                                    out=w1h[:, :].rearrange("p (a h) -> p a h", a=EH),
                                    in_=w1.rearrange("(a p) h -> p a h", p=P)[:, :, HHALF:FH]
                                ).then_inc(sem.h["ldg"], 16)
                                sem.cnt["ldg"] += 16
                                i = gp.dma_start(
                                    out=w2h[:, :].rearrange("p (a e) -> p a e", a=16),
                                    in_=w2.rearrange("(a p) e -> p a e", p=P)[:, 16:32, :])
                                sem.inc("ldg", i, 16)
                            st["wb_ld"] = sem.val("ldg")
                    elif cc <= 6:
                        for q in range(4):
                            emit_ffn_tile(1, 4 * (cc - 4) + q)
                            emit_ffn_fin()
                    else:
                        for q in (12, 13):
                            emit_ffn_tile(1, q)
                            emit_ffn_fin()

                    for s in range(NSUB):
                        gs = cc * NSUB + s
                        bs_ = s % 2
                        ts0 = s * TS
                        hloc = ts0 % HC

                        if s == NSUB // 2:
                            @block.sync
                            def _(sync, cc=cc, ct0=ct0):
                                sem.wait(sync, "sync", "vv", st["dtx_done"])
                                i = sync.dma_start(
                                    out=xnb[:, :].rearrange("p (a t) -> p a t", a=EH),
                                    in_=xn3[:, :, ct0 + HC:ct0 + CH])
                                sem.inc("lds", i, 16)
                            h2_ld = sem.val("lds")

                            @block.vector
                            def _(v, cc=cc):
                                sem.wait(v, "vector", "lds", h2_ld)
                                v.tensor_tensor(out=dtx[:, :],
                                                in0=dl_c[:, :].rearrange("p (a t) -> p a t", a=EH)[:, :, HC:CH],
                                                in1=xnb[:, :].rearrange("p (a t) -> p a t", a=EH),
                                                op=ALU.mult)
                                for chx in range(EH):
                                    i = v.tensor_scalar(out=yb[:, chx * CH + HC:(chx + 1) * CH],
                                                        in0=xnb[:, chx * HC:(chx + 1) * HC],
                                                        scalar1=D_sb[:, chx:chx + 1],
                                                        scalar2=None, op0=ALU.mult)
                                sem.inc("vv", i)
                            st["dtx_done"] = sem.val("vv")
                            st["xnb_free"] = ("vv", st["dtx_done"])

                            @block.gpsimd
                            def _(gp, cc=cc, ct0=ct0, gs=gs):
                                sem.wait(gp, "gpsimd", "vv", sub_done[gs - 1])
                                gp.dma_start(
                                    out=Bball[:, :].rearrange("p (n t) -> p n t", n=NST),
                                    in_=bc_d[0:NST, ct0 + HC:ct0 + CH].unsqueeze(0).broadcast_to([P, NST, HC])
                                ).then_inc(sem.h["ldg"], 16)
                                sem.cnt["ldg"] += 16
                                i = gp.dma_start(
                                    out=Cball[:, :].rearrange("p (n t) -> p n t", n=NST),
                                    in_=bc_d[NST:32, ct0 + HC:ct0 + CH].unsqueeze(0).broadcast_to([P, NST, HC]))
                                sem.inc("ldg", i, 16)
                            bc_ld[1] = sem.val("ldg")

                        @block.scalar
                        def _(sc, gs=gs, bs_=bs_, ts0=ts0, cc=cc):
                            sem.wait(sc, "scalar", "lds", c_ld)
                            if gs >= 2:
                                sem.wait(sc, "scalar", "vv", sub_done[gs - 2])
                            d3 = dl_c[:, :].rearrange("p (a t) -> p a t", a=EH)[:, :, ts0:ts0 + TS]
                            for n in range(NST):
                                i = sc.activation(
                                    out=dA[bs_][:, n * EH * TS:(n + 1) * EH * TS]
                                        .rearrange("p (a t) -> p a t", a=EH),
                                    in_=d3, func=AF.Exp, scale=A_bc[:, n:n + 1])
                            sem.inc("ev", i)
                        exp_done = sem.val("ev")

                        @block.vector
                        def _(v, gs=gs, bs_=bs_, ts0=ts0, hloc=hloc, s=s):
                            sem.wait(v, "vector", "ldg", bc_ld[1] if ts0 >= HC else bc_ld[0])
                            sem.wait(v, "vector", "ev", exp_done)
                            d4 = dtx[:, :].rearrange("p (a t) -> p a t", a=EH)[:, :, hloc:hloc + TS]
                            u4 = ub[:, :].rearrange("p (n a t) -> p n a t", n=NST, a=EH)
                            v.tensor_tensor(
                                out=u4,
                                in0=d4.unsqueeze(1).broadcast_to([P, NST, EH, TS]),
                                in1=Bball[:, :].rearrange("p (n t) -> p n t", n=NST)[:, :, hloc:hloc + TS]
                                    .unsqueeze(2).broadcast_to([P, NST, EH, TS]),
                                op=ALU.mult)
                            dA3 = dA[bs_][:, :].rearrange("p (g t) -> p g t", g=G)
                            v.tensor_copy(out=dA0[:, :].unsqueeze(2), in_=dA3[:, :, 0:1])
                            v.tensor_mul(tmpg[:, :], dA0[:, :], carry[:, :])
                            u3 = ub[:, :].rearrange("p (g t) -> p g t", g=G)
                            v.tensor_tensor(out=u3[:, :, 0:1], in0=u3[:, :, 0:1],
                                            in1=tmpg[:, :].unsqueeze(2), op=ALU.add)
                            v.memset(dA3[:, :, 0:1], 0.0)
                            v.tensor_tensor_scan(out=hb[:, :], data0=dA[bs_][:, :],
                                                 data1=ub[:, :], initial=0.0,
                                                 op0=ALU.mult, op1=ALU.add)
                            h3 = hb[:, :].rearrange("p (g t) -> p g t", g=G)
                            v.tensor_copy(out=carry[:, :].unsqueeze(2), in_=h3[:, :, TS - 1:TS])
                            v.tensor_tensor(
                                out=u4, in0=hb[:, :].rearrange("p (n a t) -> p n a t", n=NST, a=EH),
                                in1=Cball[:, :].rearrange("p (n t) -> p n t", n=NST)[:, :, hloc:hloc + TS]
                                    .unsqueeze(2).broadcast_to([P, NST, EH, TS]),
                                op=ALU.mult)
                            half = G * TS // 2
                            v.tensor_tensor(out=hb[:, 0:half], in0=ub[:, 0:half],
                                            in1=ub[:, half:2 * half], op=ALU.add)
                            q = half // 2
                            v.tensor_tensor(out=hb[:, half:half + q], in0=hb[:, 0:q],
                                            in1=hb[:, q:2 * q], op=ALU.add)
                            e8 = q // 2
                            v.tensor_tensor(out=hb[:, half + q:half + q + e8],
                                            in0=hb[:, half:half + e8],
                                            in1=hb[:, half + e8:half + 2 * e8], op=ALU.add)
                            e16 = e8 // 2
                            v.tensor_tensor(out=l4f[:, :], in0=hb[:, half + q:half + q + e16],
                                            in1=hb[:, half + q + e16:half + q + 2 * e16], op=ALU.add)
                            yv = yb[:, :].rearrange("p (a t) -> p a t", a=EH)[:, :, ts0:ts0 + TS]
                            i = v.tensor_tensor(
                                out=yv, in0=yv,
                                in1=l4f[:, :].rearrange("p (a t) -> p a t", a=EH),
                                op=ALU.add)
                            sem.inc("vv", i)
                        sub_done.append(sem.val("vv"))

                    y_done.append(sub_done[-1])

                emit_wout(NCHUNK - 1)
                emit_ffn_tile(1, 14)
                emit_ffn_fin()
                emit_ffn_tile(1, 15)
                emit_ffn_fin()
            barrier()
        barrier()

        @block.sync
        def _(sync):
            sem.wait(sync, "sync", "sts", sem.val("sts"))

    return nc


def kernel(L0_attn_x, L0_attn_Wx, L0_attn_A_log, L0_attn_D, L0_attn_Wout,
           L0_ffn_w1, L0_ffn_b1, L0_ffn_w2, L0_ffn_b2):
    x = np.asarray(L0_attn_x, dtype=np.float32)
    Wx = np.asarray(L0_attn_Wx, dtype=np.float32)
    A_log = np.asarray(L0_attn_A_log, dtype=np.float32)
    D = np.asarray(L0_attn_D, dtype=np.float32)
    Wout = np.asarray(L0_attn_Wout, dtype=np.float32)
    w1 = np.asarray(L0_ffn_w1, dtype=np.float32)
    b1 = np.asarray(L0_ffn_b1, dtype=np.float32)
    w2 = np.asarray(L0_ffn_w2, dtype=np.float32)
    b2 = np.asarray(L0_ffn_b2, dtype=np.float32)

    wxt = np.ascontiguousarray(np.concatenate([Wx[32:], Wx[:32]], axis=0).T)
    wot = np.ascontiguousarray(Wout.T)
    w1c = np.ascontiguousarray(w1)
    w2c = np.ascontiguousarray(w2)
    alog = np.ascontiguousarray(A_log.reshape(1, 16))
    dvec = np.ascontiguousarray(D.reshape(8, 128).T)
    b1m = np.ascontiguousarray(b1.reshape(32, 128).T)
    b2m = np.ascontiguousarray(b2.reshape(8, 128).T)

    nc = build_program()
    in_maps = []
    for i in range(8):
        in_maps.append({
            "xin": np.ascontiguousarray(x[i].T),
            "wxt": wxt, "w1": w1c, "w2": w2c, "wot": wot,
            "alog": alog, "dvec": dvec, "b1m": b1m, "b2m": b2m,
        })
    res = run_bass_kernel_spmd(nc, in_maps, core_ids=list(range(8)))
    out = np.stack([np.asarray(r["yout"], dtype=np.float32).T for r in res.results], axis=0)
    return np.ascontiguousarray(out)


# version 13
# speedup vs baseline: 1.1136x; 1.1136x over previous
import numpy as np
from contextlib import ExitStack

import concourse.bass as bass
import concourse.mybir as mybir
from concourse.bass_utils import run_bass_kernel_spmd

F32 = mybir.dt.float32
BF16 = mybir.dt.bfloat16
AF = mybir.ActivationFunctionType
ALU = mybir.AluOpType

P = 128
T = 4096
EH = 8
H = 1024
NST = 16
J = 1056
FH = 4096
CH = 512
NCHUNK = T // CH
TS = 32
NSUB = CH // TS
TT = 512
NTILE = T // TT
G = 128
EPS = 1e-5


class Sems:

    def __init__(self, nc, stack, names):
        self.h = {n: stack.enter_context(nc.semaphore(n)) for n in names}
        self.cnt = {n: 0 for n in names}
        self.waited = {}

    def inc(self, name, instr, k=1):
        instr.then_inc(self.h[name], k)
        self.cnt[name] += k
        return self.cnt[name]

    def val(self, name):
        return self.cnt[name]

    def wait(self, eng, ename, name, target):
        if target is None or target <= 0:
            return
        key = (ename, name)
        if self.waited.get(key, 0) >= target:
            return
        eng.wait_ge(self.h[name], target)
        self.waited[key] = target


def build_program(phases=4, dbg=False):
    SCR = "ExternalOutput"
    nc = bass.Bass("TRN2", target_bir_lowering=False, debug=False)

    xin = nc.dram_tensor("xin", [H, T], F32, kind="ExternalInput").ap()
    wxt = nc.dram_tensor("wxt", [H, J], F32, kind="ExternalInput").ap()
    w1 = nc.dram_tensor("w1", [H, FH], F32, kind="ExternalInput").ap()
    w2 = nc.dram_tensor("w2", [FH, H], F32, kind="ExternalInput").ap()
    wot = nc.dram_tensor("wot", [H, H], F32, kind="ExternalInput").ap()
    alog = nc.dram_tensor("alog", [1, NST], F32, kind="ExternalInput").ap()
    dvec = nc.dram_tensor("dvec", [P, EH], F32, kind="ExternalInput").ap()
    b1m = nc.dram_tensor("b1m", [P, 32], F32, kind="ExternalInput").ap()
    b2m = nc.dram_tensor("b2m", [P, EH], F32, kind="ExternalInput").ap()
    yout = nc.dram_tensor("yout", [H, T], F32, kind="ExternalOutput").ap()
    a_d = nc.dram_tensor("a_d", [1, NST], F32, kind=SCR).ap()
    ab_d = nc.dram_tensor("ab_d", [2, T], F32, kind=SCR).ap()
    bc_d = nc.dram_tensor("bc_d", [32, T], F32, kind=SCR).ap()
    xn_d = nc.dram_tensor("xn_d", [H, T], BF16, kind=SCR).ap()
    delta_d = nc.dram_tensor("delta_d", [H, T], BF16, kind=SCR).ap()
    attn_d = nc.dram_tensor("attn_d", [H, T], BF16, kind=SCR).ap()
    acca_d = nc.dram_tensor("acca_d", [H, T], BF16, kind=SCR).ap()

    xin3 = xin.rearrange("(a p) t -> p a t", p=P)
    xn3 = xn_d.rearrange("(a p) t -> p a t", p=P)
    delta3 = delta_d.rearrange("(a p) t -> p a t", p=P)
    attn3 = attn_d.rearrange("(a p) t -> p a t", p=P)
    acca3 = acca_d.rearrange("(a p) t -> p a t", p=P)
    yout3 = yout.rearrange("(a p) t -> p a t", p=P)

    with ExitStack() as outer:
        sem = Sems(nc, outer, [
            "lds",
            "sts",
            "ldg",
            "mm",
            "ev",
            "vv",
            "bs", "bg", "bt", "bc_", "bv",
        ])

        def sb(stack, name, shape, dt):
            return stack.enter_context(nc.sbuf_tensor(name, shape, dt))

        A_bc = sb(outer, "A_bc", [P, NST], F32)
        D_sb = sb(outer, "D_sb", [P, EH], F32)
        B1_sb = sb(outer, "B1_sb", [P, 32], F32)
        B2_sb = sb(outer, "B2_sb", [P, EH], F32)
        ones = sb(outer, "ones", [P, 1], F32)
        arow = sb(outer, "arow", [1, NST], F32)

        ps = [outer.enter_context(nc.psum_tensor(f"ps{i}", [P, 512], F32))
              for i in range(6)]
        bank_free = {i: None for i in range(6)}

        block = outer.enter_context(nc.Block())
        barrier_n = [0]

        def barrier():
            barrier_n[0] += 1
            k = barrier_n[0]
            pairs = [("sync", "bs"), ("gpsimd", "bg"), ("tensor", "bt"),
                     ("scalar", "bc_"), ("vector", "bv")]
            for ename, s in pairs:
                @getattr(block, ename)
                def _(eng, s=s):
                    eng.sem_inc(sem.h[s], 1)
                sem.cnt[s] += 1
            for ename, _s in pairs:
                @getattr(block, ename)
                def _(eng, ename=ename):
                    for _, s2 in pairs:
                        sem.wait(eng, ename, s2, k)

        with ExitStack() as s13:
            Wot = sb(s13, "Wot", [P, EH * H], BF16)

            @block.sync
            def _(sync):
                sync.dma_start(out=arow[:, :], in_=alog[:, :]).then_inc(sem.h["lds"], 16)
                sem.cnt["lds"] += 16
                sync.dma_start(out=D_sb[:, :], in_=dvec[:, :]).then_inc(sem.h["lds"], 16)
                sem.cnt["lds"] += 16
                sync.dma_start(out=B1_sb[:, :], in_=b1m[:, :]).then_inc(sem.h["lds"], 16)
                sem.cnt["lds"] += 16
                sync.dma_start(out=B2_sb[:, :], in_=b2m[:, :]).then_inc(sem.h["lds"], 16)
                sem.cnt["lds"] += 16
            ph0_lds = sem.val("lds")

            @block.gpsimd
            def _(gp):
                i = gp.dma_start(out=Wot[:, :].rearrange("p (a e) -> p a e", a=EH),
                                 in_=wot.rearrange("(a p) e -> p a e", p=P))
                sem.inc("ldg", i, 16)
            wx_ld = sem.val("ldg")

            @block.scalar
            def _(sc):
                sem.wait(sc, "scalar", "lds", ph0_lds)
                i = sc.activation(out=arow[:, :], in_=arow[:, :], func=AF.Exp)
                sem.inc("ev", i)
            a_exp = sem.val("ev")

            @block.vector
            def _(v):
                v.memset(ones[:, :], 1.0)
                sem.wait(v, "vector", "ev", a_exp)
                i = v.tensor_scalar(out=arow[:, :], in0=arow[:, :], scalar1=-1.0,
                                    scalar2=None, op0=ALU.mult)
                sem.inc("vv", i)
            a_neg = sem.val("vv")

            @block.sync
            def _(sync):
                sem.wait(sync, "sync", "vv", a_neg)
                i = sync.dma_start(out=a_d[:, :], in_=arow[:, :])
                sem.inc("sts", i, 16)
            a_spill = sem.val("sts")

            @block.gpsimd
            def _(gp):
                sem.wait(gp, "gpsimd", "sts", a_spill)
                i = gp.dma_start(out=A_bc[:, :], in_=a_d[0:1, :].broadcast_to([P, NST]))
                sem.inc("ldg", i, 16)
            abc_ld = sem.val("ldg")

            with ExitStack() as s1:
                WxT = sb(s1, "WxT", [P, EH * J], BF16)
                xt = [sb(s1, f"xt{b}", [P, EH * TT], F32) for b in range(2)]
                xsq = sb(s1, "xsq", [P, EH * TT], F32)
                xn_t = [sb(s1, f"xnt{b}", [P, EH * TT], BF16) for b in range(2)]
                dl_t = [sb(s1, f"dlt{b}", [P, EH * TT], BF16) for b in range(2)]
                BCs = sb(s1, "BCs", [32, TT], F32)
                a_bc = sb(s1, "a_bc", [P, TT], F32)
                b_bc = sb(s1, "b_bc", [P, TT], F32)
                rows = sb(s1, "rows", [1, 512 * 10], F32)

                def row(i):
                    return rows[:, i * 512:(i + 1) * 512]

                @block.gpsimd
                def _(gp):
                    i = gp.dma_start(out=WxT[:, :].rearrange("p (a j) -> p a j", a=EH),
                                     in_=wxt.rearrange("(a p) j -> p a j", p=P))
                    sem.inc("ldg", i, 16)
                wxt_ld = sem.val("ldg")

                x_loaded = []
                xt_free = [None, None]
                xn_free = [None, None]
                xsq_free = None
                abc_free = None
                ab_sp_prev = None
                bc_sp_prev = None
                bc_sp = []
                dl_free = [None, None]
                spill_cnt = []
                dl_spill = []
                dlb_evcnt = {}

                spsc = sb(s1, "spsc", [P, TT], F32)
                xn_done_t = {}

                def stage_a(tt):
                    nonlocal abc_free, ab_sp_prev
                    b = tt % 2
                    t0 = tt * TT

                    @block.sync
                    def _(sync, b=b, t0=t0):
                        if xt_free[b] is not None:
                            sem.wait(sync, "sync", *xt_free[b])
                        i = sync.dma_start(out=xt[b][:, :].rearrange("p (a t) -> p a t", a=EH),
                                           in_=xin3[:, :, t0:t0 + TT])
                        sem.inc("lds", i, 16)
                    x_loaded.append(sem.val("lds"))

                    @block.scalar
                    def _(sc, b=b):
                        sem.wait(sc, "scalar", "lds", x_loaded[-1])
                        if xsq_free[0] is not None:
                            sem.wait(sc, "scalar", *xsq_free[0])
                        i = sc.activation(out=xsq[:, :], in_=xt[b][:, :], func=AF.Square)
                        sem.inc("ev", i)
                    xsq_done = sem.val("ev")

                    @block.tensor
                    def _(te, b=b):
                        sem.wait(te, "tensor", "lds", x_loaded[-1])
                        sem.wait(te, "tensor", "ev", xsq_done)
                        if bank_free[4] is not None:
                            sem.wait(te, "tensor", *bank_free[4])
                        if bank_free[5] is not None:
                            sem.wait(te, "tensor", *bank_free[5])
                        for kc in range(EH):
                            te.matmul(out=ps[4][0:1, 0:TT], lhsT=ones[:, :],
                                      rhs=xt[b][:, kc * TT:(kc + 1) * TT],
                                      start=(kc == 0), stop=(kc == EH - 1))
                        for kc in range(EH):
                            i = te.matmul(out=ps[5][0:1, 0:TT], lhsT=ones[:, :],
                                          rhs=xsq[:, kc * TT:(kc + 1) * TT],
                                          start=(kc == 0), stop=(kc == EH - 1))
                        sem.inc("mm", i)
                    stats_done = sem.val("mm")
                    xsq_free[0] = ("mm", stats_done)

                    @block.vector
                    def _(v):
                        sem.wait(v, "vector", "mm", stats_done)
                        if ab_sp_prev is not None:
                            sem.wait(v, "vector", "sts", ab_sp_prev)
                        v.tensor_scalar(out=row(0), in0=ps[4][0:1, 0:TT],
                                        scalar1=1.0 / H, scalar2=None, op0=ALU.mult)
                        v.tensor_scalar(out=row(1), in0=ps[5][0:1, 0:TT],
                                        scalar1=1.0 / H, scalar2=None, op0=ALU.mult)
                        v.tensor_mul(row(2), row(0), row(0))
                        v.tensor_sub(row(3), row(1), row(2))
                        i = v.tensor_scalar(out=row(4), in0=row(3), scalar1=EPS,
                                            scalar2=None, op0=ALU.add)
                        sem.inc("vv", i)
                    veps_done = sem.val("vv")
                    bank_free[4] = ("vv", veps_done)
                    bank_free[5] = ("vv", veps_done)

                    @block.scalar
                    def _(sc):
                        sem.wait(sc, "scalar", "vv", veps_done)
                        i = sc.activation(out=row(5), in_=row(4), func=AF.Sqrt)
                        sem.inc("ev", i)
                    sq_done = sem.val("ev")

                    @block.vector
                    def _(v):
                        sem.wait(v, "vector", "ev", sq_done)
                        v.reciprocal(row(6), row(5))
                        v.tensor_mul(row(2), row(6), row(6))
                        v.tensor_mul(row(1), row(2), row(4))
                        v.tensor_scalar(out=row(2), in0=row(1), scalar1=-0.5,
                                        scalar2=1.5, op0=ALU.mult, op1=ALU.add)
                        v.tensor_mul(row(7), row(6), row(2))
                        v.tensor_mul(row(8), row(0), row(7))
                        i = v.tensor_scalar(out=row(8), in0=row(8), scalar1=-1.0,
                                            scalar2=None, op0=ALU.mult)
                        sem.inc("vv", i)
                    ab_done = sem.val("vv")

                    @block.sync
                    def _(sync, t0=t0):
                        sem.wait(sync, "sync", "vv", ab_done)
                        sync.dma_start(out=ab_d[0:1, t0:t0 + TT], in_=row(7)).then_inc(sem.h["sts"], 16)
                        sem.cnt["sts"] += 16
                        i = sync.dma_start(out=ab_d[1:2, t0:t0 + TT], in_=row(8))
                        sem.inc("sts", i, 16)
                    ab_sp = sem.val("sts")
                    ab_sp_prev = ab_sp

                    @block.gpsimd
                    def _(gp, t0=t0):
                        sem.wait(gp, "gpsimd", "sts", ab_sp)
                        if abc_free is not None:
                            sem.wait(gp, "gpsimd", *abc_free)
                        gp.dma_start(out=a_bc[:, :],
                                     in_=ab_d[0:1, t0:t0 + TT].broadcast_to([P, TT])).then_inc(sem.h["ldg"], 16)
                        sem.cnt["ldg"] += 16
                        i = gp.dma_start(out=b_bc[:, :],
                                         in_=ab_d[1:2, t0:t0 + TT].broadcast_to([P, TT]))
                        sem.inc("ldg", i, 16)
                    abc_ld2 = sem.val("ldg")

                    @block.vector
                    def _(v, b=b):
                        sem.wait(v, "vector", "ldg", abc_ld2)
                        if xn_free[b] is not None:
                            for cond in xn_free[b]:
                                sem.wait(v, "vector", *cond)
                        v.tensor_tensor(
                            out=xt[b][:, :].rearrange("p (a t) -> p a t", a=EH),
                            in0=xt[b][:, :].rearrange("p (a t) -> p a t", a=EH),
                            in1=a_bc[:, :].unsqueeze(1).broadcast_to([P, EH, TT]),
                            op=ALU.mult)
                        i = v.tensor_tensor(
                            out=xn_t[b][:, :].rearrange("p (a t) -> p a t", a=EH),
                            in0=xt[b][:, :].rearrange("p (a t) -> p a t", a=EH),
                            in1=b_bc[:, :].unsqueeze(1).broadcast_to([P, EH, TT]),
                            op=ALU.add)
                        sem.inc("vv", i)
                    xn_done_t[tt] = sem.val("vv")
                    xt_free[b] = ("vv", xn_done_t[tt])
                    abc_free = ("vv", xn_done_t[tt])

                    @block.sync
                    def _(sync, b=b, t0=t0, tt=tt):
                        sem.wait(sync, "sync", "vv", xn_done_t[tt])
                        i = sync.dma_start(out=xn3[:, :, t0:t0 + TT],
                                           in_=xn_t[b][:, :].rearrange("p (a t) -> p a t", a=EH))
                        sem.inc("sts", i, 16)
                    spill_cnt.append(sem.val("sts"))

                def stage_b(tt):
                    nonlocal bc_sp_prev
                    b = tt % 2
                    t0 = tt * TT
                    ev_base2 = sem.val("ev")
                    for jb in [8] + list(range(8)):
                        bank = 2 if jb == 8 else jb % 2
                        mwid = 32 if jb == 8 else P

                        @block.tensor
                        def _(te, b=b, tt=tt, jb=jb, bank=bank, mwid=mwid):
                            sem.wait(te, "tensor", "vv", xn_done_t[tt])
                            sem.wait(te, "tensor", "ldg", wxt_ld)
                            if jb in (8, 0, 1):
                                if bank_free[bank] is not None:
                                    sem.wait(te, "tensor", *bank_free[bank])
                            else:
                                sem.wait(te, "tensor", "ev", ev_base2 + jb)
                            for kc in range(EH):
                                i = te.matmul(
                                    out=ps[bank][0:mwid, 0:TT],
                                    lhsT=WxT[:, kc * J + jb * P:kc * J + jb * P + mwid],
                                    rhs=xn_t[b][:, kc * TT:(kc + 1) * TT],
                                    start=(kc == 0), stop=(kc == EH - 1))
                            sem.inc("mm", i)
                            dlb_evcnt[(tt, jb)] = sem.val("mm")
                    wx_done = sem.val("mm")
                    xn_free[b] = [("sts", spill_cnt[tt]), ("mm", wx_done)]

                    @block.scalar
                    def _(sc, b=b, tt=tt):
                        sem.wait(sc, "scalar", "mm", dlb_evcnt[(tt, 8)])
                        if bc_sp_prev is not None:
                            sem.wait(sc, "scalar", "sts", bc_sp_prev)
                        i = sc.activation(out=BCs[:, :], in_=ps[2][0:32, 0:TT],
                                          func=AF.Copy)
                        sem.inc("ev", i)
                        if dl_free[b] is not None:
                            sem.wait(sc, "scalar", *dl_free[b])
                        for jb in range(8):
                            bank = jb % 2
                            sem.wait(sc, "scalar", "mm", dlb_evcnt[(tt, jb)])
                            sc.activation(out=spsc[:, :], in_=ps[bank][:, 0:TT],
                                          func=AF.Exp)
                            i = sc.activation(out=dl_t[b][:, jb * TT:(jb + 1) * TT],
                                              in_=spsc[:, :], func=AF.Ln, bias=1.0)
                            sem.inc("ev", i)
                    assert sem.val("ev") == ev_base2 + 9
                    bank_free[2] = ("ev", ev_base2 + 1)
                    bank_free[0] = ("ev", ev_base2 + 8)
                    bank_free[1] = ("ev", ev_base2 + 9)
                    soft_done = sem.val("ev")

                    @block.sync
                    def _(sync, b=b, t0=t0, tt=tt):
                        sem.wait(sync, "sync", "ev", soft_done)
                        sync.dma_start(out=bc_d[:, t0:t0 + TT],
                                       in_=BCs[:, :]).then_inc(sem.h["sts"], 16)
                        sem.cnt["sts"] += 16
                        i = sync.dma_start(out=delta3[:, :, t0:t0 + TT],
                                           in_=dl_t[b][:, :].rearrange("p (a t) -> p a t", a=EH))
                        sem.inc("sts", i, 16)
                    dl_spill.append(sem.val("sts"))
                    bc_sp.append(sem.val("sts"))
                    bc_sp_prev = sem.val("sts")
                    dl_free[b] = ("sts", dl_spill[-1])

                xsq_free = [None]
                stage_a(0)
                stage_a(1)
                for tt in range(NTILE):
                    stage_b(tt)
                    if tt + 2 < NTILE:
                        stage_a(tt + 2)

            barrier()

            if phases < 3:
                @block.sync
                def _(sync):
                    sem.wait(sync, "sync", "sts", sem.val("sts"))
                return nc
            with ExitStack() as s3:
                HC = CH // 2
                xnb = sb(s3, "xnb", [P, EH * HC], BF16)
                dl_c = sb(s3, "dlc", [P, EH * CH], BF16)
                dtx = sb(s3, "dtx", [P, EH * HC], BF16)
                dA = [sb(s3, f"dA{b}", [P, G * TS], BF16) for b in range(2)]
                ub = sb(s3, "ub", [P, G * TS], BF16)
                hb = sb(s3, "hb", [P, G * TS], BF16)
                l4f = sb(s3, "l4f", [P, EH * TS], F32)
                yb = sb(s3, "yb", [P, EH * CH], BF16)
                att_t = sb(s3, "att_t", [P, EH * CH], BF16)
                carry = sb(s3, "carry", [P, G], F32)
                dA0 = sb(s3, "dA0v", [P, G], F32)
                tmpg = sb(s3, "tmpg", [P, G], F32)
                Bball = sb(s3, "Bball", [P, NST * HC], BF16)
                Cball = sb(s3, "Cball", [P, NST * HC], BF16)
                HHALF = FH // 2
                FT = 256
                NFT = T // FT
                xnf = sb(s3, "xnf", [P, EH * FT], BF16)
                w1h = sb(s3, "w1h", [P, EH * HHALF], BF16)
                w2h = sb(s3, "w2h", [P, 16 * H], BF16)
                resf = sb(s3, "resf", [P, EH * FT], F32)
                attf = sb(s3, "attf", [P, EH * FT], BF16)
                acb = sb(s3, "acb", [P, EH * FT], BF16)
                hf = sb(s3, "hf", [P, 16 * FT], BF16)
                outf = sb(s3, "outf", [P, EH * FT], F32)

                @block.vector
                def _(v):
                    v.memset(carry[:, :], 0.0)

                @block.gpsimd
                def _(gp):
                    gp.dma_start(
                        out=w1h[:, :].rearrange("p (a h) -> p a h", a=EH),
                        in_=w1.rearrange("(a p) h -> p a h", p=P)[:, :, 0:HHALF]
                    ).then_inc(sem.h["ldg"], 16)
                    sem.cnt["ldg"] += 16
                    i = gp.dma_start(
                        out=w2h[:, :].rearrange("p (a e) -> p a e", a=16),
                        in_=w2.rearrange("(a p) e -> p a e", p=P)[:, 0:16, :])
                    sem.inc("ldg", i, 16)
                w_ld = sem.val("ldg")

                sub_done = []
                y_done = []
                att_sp = []
                acca_sp = {}
                dlb_evcnt = {}
                st = {"xnb_free": None, "dl_free": None, "yb_free": None,
                      "att_free": None, "resf_free": None, "attf_free": None,
                      "outf_free": None, "hf_free": None, "bc_free": None,
                      "passA_mm": None, "dtx_done": 0, "xnf_free": None,
                      "wb_ld": None, "acb_free": None}
                pend_fin = []

                def emit_wout(cc):
                    evw_base = sem.val("ev")
                    for eb in range(EH):
                        bank = 2 + eb % 2

                        @block.tensor
                        def _(te, cc=cc, eb=eb, bank=bank):
                            sem.wait(te, "tensor", "vv", y_done[cc])
                            sem.wait(te, "tensor", "ldg", wx_ld)
                            if eb < 2:
                                if bank_free[bank] is not None:
                                    sem.wait(te, "tensor", *bank_free[bank])
                            else:
                                sem.wait(te, "tensor", "ev", evw_base + eb - 1)
                            for kc in range(EH):
                                i = te.matmul(
                                    out=ps[bank][:, 0:CH],
                                    lhsT=Wot[:, kc * H + eb * P:kc * H + (eb + 1) * P],
                                    rhs=yb[:, kc * CH:(kc + 1) * CH],
                                    start=(kc == 0), stop=(kc == EH - 1))
                            sem.inc("mm", i)
                            dlb_evcnt[("wo", cc, eb)] = sem.val("mm")

                    @block.scalar
                    def _(sc, cc=cc):
                        if att_sp:
                            sem.wait(sc, "scalar", "sts", att_sp[-1])
                        for eb in range(EH):
                            bank = 2 + eb % 2
                            sem.wait(sc, "scalar", "mm", dlb_evcnt[("wo", cc, eb)])
                            i = sc.activation(out=att_t[:, eb * CH:(eb + 1) * CH],
                                              in_=ps[bank][:, 0:CH], func=AF.Copy)
                            sem.inc("ev", i)
                    assert sem.val("ev") == evw_base + EH
                    bank_free[2] = ("ev", evw_base + EH - 1)
                    bank_free[3] = ("ev", evw_base + EH)
                    wev = sem.val("ev")

                    @block.sync
                    def _(sync, cc=cc, wev=wev):
                        sem.wait(sync, "sync", "ev", wev)
                        i = sync.dma_start(out=attn3[:, :, cc * CH:(cc + 1) * CH],
                                           in_=att_t[:, :].rearrange("p (a t) -> p a t", a=EH))
                        sem.inc("sts", i, 16)
                    att_sp.append(sem.val("sts"))
                    st["yb_free"] = ("mm", dlb_evcnt[("wo", cc, EH - 1)])

                def emit_ffn_tile(pa, tf):
                    t0 = tf * FT

                    @block.sync
                    def _(sync, pa=pa, tf=tf, t0=t0):
                        if st["xnf_free"] is not None:
                            sem.wait(sync, "sync", *st["xnf_free"])
                        sem.wait(sync, "sync", "sts", spill_cnt[tf // 2])
                        i = sync.dma_start(out=xnf[:, :].rearrange("p (a t) -> p a t", a=EH),
                                           in_=xn3[:, :, t0:t0 + FT])
                        sem.inc("lds", i, 16)
                        if pa == 1:
                            if st["resf_free"] is not None:
                                sem.wait(sync, "sync", *st["resf_free"])
                            sync.dma_start(out=resf[:, :].rearrange("p (a t) -> p a t", a=EH),
                                           in_=xin3[:, :, t0:t0 + FT]).then_inc(sem.h["lds"], 16)
                            sem.cnt["lds"] += 16
                            if st["acb_free"] is not None:
                                sem.wait(sync, "sync", *st["acb_free"])
                            sem.wait(sync, "sync", "sts", acca_sp[tf])
                            sync.dma_start(out=acb[:, :].rearrange("p (a t) -> p a t", a=EH),
                                           in_=acca3[:, :, t0:t0 + FT]).then_inc(sem.h["lds"], 16)
                            sem.cnt["lds"] += 16
                            if st["attf_free"] is not None:
                                sem.wait(sync, "sync", *st["attf_free"])
                            sem.wait(sync, "sync", "sts", att_sp[tf // 2])
                            i = sync.dma_start(out=attf[:, :].rearrange("p (a t) -> p a t", a=EH),
                                               in_=attn3[:, :, t0:t0 + FT])
                            sem.inc("lds", i, 16)
                    f_ld = sem.val("lds")

                    evt_base = sem.val("ev")
                    for hb_ in range(16):
                        bank = hb_ % 2

                        @block.tensor
                        def _(te, pa=pa, hb_=hb_, bank=bank):
                            sem.wait(te, "tensor", "lds", f_ld)
                            sem.wait(te, "tensor", "ldg",
                                     w_ld if pa == 0 else st["wb_ld"])
                            if hb_ == 0 and st["hf_free"] is not None:
                                sem.wait(te, "tensor", *st["hf_free"])
                            if hb_ < 2:
                                if bank_free[bank] is not None:
                                    sem.wait(te, "tensor", *bank_free[bank])
                            else:
                                sem.wait(te, "tensor", "ev", evt_base + hb_ - 1)
                            for kc in range(EH):
                                i = te.matmul(
                                    out=ps[bank][:, 0:FT],
                                    lhsT=w1h[:, kc * HHALF + hb_ * P:kc * HHALF + (hb_ + 1) * P],
                                    rhs=xnf[:, kc * FT:(kc + 1) * FT],
                                    start=(kc == 0), stop=(kc == EH - 1))
                            sem.inc("mm", i)
                            dlb_evcnt[("f1", pa, tf, hb_)] = sem.val("mm")

                    @block.scalar
                    def _(sc, pa=pa, tf=tf):
                        for hb_ in range(16):
                            bank = hb_ % 2
                            sem.wait(sc, "scalar", "mm", dlb_evcnt[("f1", pa, tf, hb_)])
                            i = sc.activation(out=hf[:, hb_ * FT:(hb_ + 1) * FT],
                                              in_=ps[bank][:, 0:FT], func=AF.Gelu,
                                              bias=B1_sb[:, pa * 16 + hb_:pa * 16 + hb_ + 1])
                            sem.inc("ev", i)
                    assert sem.val("ev") == evt_base + 16
                    bank_free[0] = ("ev", evt_base + 15)
                    bank_free[1] = ("ev", evt_base + 16)
                    gelu_done = sem.val("ev")

                    evq_base = sem.val("ev")
                    for eb in range(EH):
                        bank = 4 + eb % 2

                        @block.tensor
                        def _(te, pa=pa, eb=eb, bank=bank):
                            sem.wait(te, "tensor", "ev", gelu_done)
                            if eb < 2:
                                if bank_free[bank] is not None:
                                    sem.wait(te, "tensor", *bank_free[bank])
                            else:
                                sem.wait(te, "tensor", "ev", evq_base + eb - 1)
                            for kc in range(16):
                                i = te.matmul(
                                    out=ps[bank][:, 0:FT],
                                    lhsT=w2h[:, kc * H + eb * P:kc * H + (eb + 1) * P],
                                    rhs=hf[:, kc * FT:(kc + 1) * FT],
                                    start=(kc == 0), stop=(kc == 15))
                            sem.inc("mm", i)
                            dlb_evcnt[("f2", pa, tf, eb)] = sem.val("mm")
                    st["hf_free"] = ("mm", dlb_evcnt[("f2", pa, tf, EH - 1)])
                    st["xnf_free"] = ("mm", dlb_evcnt[("f1", pa, tf, 15)])

                    @block.scalar
                    def _(sc, pa=pa, tf=tf):
                        if pa == 0 and st["attf_free"] is not None:
                            sem.wait(sc, "scalar", *st["attf_free"])
                        if pa == 1 and st["outf_free"] is not None:
                            sem.wait(sc, "scalar", *st["outf_free"])
                        for eb in range(EH):
                            bank = 4 + eb % 2
                            sem.wait(sc, "scalar", "mm", dlb_evcnt[("f2", pa, tf, eb)])
                            if pa == 0:
                                i = sc.activation(out=attf[:, eb * FT:(eb + 1) * FT],
                                                  in_=ps[bank][:, 0:FT], func=AF.Copy)
                            else:
                                i = sc.activation(out=outf[:, eb * FT:(eb + 1) * FT],
                                                  in_=ps[bank][:, 0:FT], func=AF.Identity,
                                                  bias=B2_sb[:, eb:eb + 1])
                            sem.inc("ev", i)
                    assert sem.val("ev") == evq_base + EH
                    bank_free[4] = ("ev", evq_base + EH - 1)
                    bank_free[5] = ("ev", evq_base + EH)
                    fev = sem.val("ev")

                    if pa == 0:
                        @block.sync
                        def _(sync, tf=tf, t0=t0, fev=fev):
                            sem.wait(sync, "sync", "ev", fev)
                            i = sync.dma_start(out=acca3[:, :, t0:t0 + FT],
                                               in_=attf[:, :].rearrange("p (a t) -> p a t", a=EH))
                            sem.inc("sts", i, 16)
                        acca_sp[tf] = sem.val("sts")
                        st["attf_free"] = ("sts", acca_sp[tf])
                        st["passA_mm"] = st["hf_free"]
                        pend_fin.append(None)
                    else:
                        pend_fin.append((tf, t0, fev, f_ld))

                def emit_ffn_fin():
                    while pend_fin:
                        item = pend_fin.pop(0)
                        if item is None:
                            continue
                        tf, t0, fev, f_ld = item

                        @block.vector
                        def _(v, fev=fev, f_ld=f_ld):
                            sem.wait(v, "vector", "ev", fev)
                            sem.wait(v, "vector", "lds", f_ld)
                            v.tensor_tensor(out=outf[:, :], in0=outf[:, :],
                                            in1=resf[:, :], op=ALU.add)
                            v.tensor_tensor(out=outf[:, :], in0=outf[:, :],
                                            in1=acb[:, :], op=ALU.add)
                            i = v.tensor_tensor(out=outf[:, :], in0=outf[:, :],
                                                in1=attf[:, :], op=ALU.add)
                            sem.inc("vv", i)
                        fin = sem.val("vv")
                        st["resf_free"] = ("vv", fin)
                        st["attf_free"] = ("vv", fin)
                        st["acb_free"] = ("vv", fin)

                        @block.sync
                        def _(sync, tf=tf, t0=t0, fin=fin):
                            sem.wait(sync, "sync", "vv", fin)
                            i = sync.dma_start(out=yout3[:, :, t0:t0 + FT],
                                               in_=outf[:, :].rearrange("p (a t) -> p a t", a=EH))
                            sem.inc("sts", i, 16)
                        st["outf_free"] = ("sts", sem.val("sts"))

                for cc in range(NCHUNK):
                    ct0 = cc * CH

                    @block.sync
                    def _(sync, cc=cc, ct0=ct0):
                        sem.wait(sync, "sync", "sts", dl_spill[cc])
                        if st["xnb_free"] is not None:
                            sem.wait(sync, "sync", *st["xnb_free"])
                        if cc >= 1:
                            sem.wait(sync, "sync", "vv", sub_done[cc * NSUB - 1])
                        sync.dma_start(out=xnb[:, :].rearrange("p (a t) -> p a t", a=EH),
                                       in_=xn3[:, :, ct0:ct0 + HC]).then_inc(sem.h["lds"], 16)
                        sem.cnt["lds"] += 16
                        i = sync.dma_start(out=dl_c[:, :].rearrange("p (a t) -> p a t", a=EH),
                                           in_=delta3[:, :, ct0:ct0 + CH])
                        sem.inc("lds", i, 16)
                    c_ld = sem.val("lds")

                    @block.gpsimd
                    def _(gp, cc=cc, ct0=ct0):
                        sem.wait(gp, "gpsimd", "sts", bc_sp[cc])
                        if cc >= 1:
                            sem.wait(gp, "gpsimd", "vv", sub_done[cc * NSUB - 1])
                        for pq in range(4):
                            gp.dma_start(
                                out=Bball[pq * 32:(pq + 1) * 32, :].rearrange("p (n t) -> p n t", n=NST),
                                in_=bc_d[0:NST, ct0:ct0 + HC].unsqueeze(0).broadcast_to([32, NST, HC])
                            ).then_inc(sem.h["ldg"], 16)
                            sem.cnt["ldg"] += 16
                            i = gp.dma_start(
                                out=Cball[pq * 32:(pq + 1) * 32, :].rearrange("p (n t) -> p n t", n=NST),
                                in_=bc_d[NST:32, ct0:ct0 + HC].unsqueeze(0).broadcast_to([32, NST, HC]))
                            sem.inc("ldg", i, 16)
                    bc_ld = [sem.val("ldg"), None]

                    if cc >= 1:
                        emit_wout(cc - 1)

                    @block.vector
                    def _(v, cc=cc):
                        sem.wait(v, "vector", "lds", c_ld)
                        if st["yb_free"] is not None:
                            sem.wait(v, "vector", *st["yb_free"])
                        v.tensor_tensor(out=dtx[:, :],
                                        in0=dl_c[:, :].rearrange("p (a t) -> p a t", a=EH)[:, :, 0:HC],
                                        in1=xnb[:, :].rearrange("p (a t) -> p a t", a=EH),
                                        op=ALU.mult)
                        for chx in range(EH):
                            i = v.tensor_scalar(out=yb[:, chx * CH:chx * CH + HC],
                                                in0=xnb[:, chx * HC:(chx + 1) * HC],
                                                scalar1=D_sb[:, chx:chx + 1],
                                                scalar2=None, op0=ALU.mult)
                        sem.inc("vv", i)
                    st["dtx_done"] = sem.val("vv")
                    st["xnb_free"] = ("vv", st["dtx_done"])

                    if cc <= 3:
                        ffn_jobs = [(0, 4 * cc + q) for q in range(4)]
                    elif cc <= 6:
                        ffn_jobs = [(1, 4 * (cc - 4) + q) for q in range(4)]
                    else:
                        ffn_jobs = [(1, 12), (1, 13)]

                    def emit_ffn_batch(jobs):
                        for pa_, tf_ in jobs:
                            emit_ffn_tile(pa_, tf_)
                            emit_ffn_fin()
                        if cc == 3 and jobs and jobs[-1][1] == 15:
                            @block.gpsimd
                            def _(gp):
                                sem.wait(gp, "gpsimd", *st["passA_mm"])
                                gp.dma_start(
                                    out=w1h[:, :].rearrange("p (a h) -> p a h", a=EH),
                                    in_=w1.rearrange("(a p) h -> p a h", p=P)[:, :, HHALF:FH]
                                ).then_inc(sem.h["ldg"], 16)
                                sem.cnt["ldg"] += 16
                                i = gp.dma_start(
                                    out=w2h[:, :].rearrange("p (a e) -> p a e", a=16),
                                    in_=w2.rearrange("(a p) e -> p a e", p=P)[:, 16:32, :])
                                sem.inc("ldg", i, 16)
                            st["wb_ld"] = sem.val("ldg")

                    for s in range(NSUB):
                        gs = cc * NSUB + s
                        bs_ = s % 2
                        ts0 = s * TS
                        hloc = ts0 % HC

                        if s == NSUB // 2:
                            @block.sync
                            def _(sync, cc=cc, ct0=ct0):
                                sem.wait(sync, "sync", "vv", st["dtx_done"])
                                i = sync.dma_start(
                                    out=xnb[:, :].rearrange("p (a t) -> p a t", a=EH),
                                    in_=xn3[:, :, ct0 + HC:ct0 + CH])
                                sem.inc("lds", i, 16)
                            h2_ld = sem.val("lds")

                            @block.vector
                            def _(v, cc=cc):
                                sem.wait(v, "vector", "lds", h2_ld)
                                v.tensor_tensor(out=dtx[:, :],
                                                in0=dl_c[:, :].rearrange("p (a t) -> p a t", a=EH)[:, :, HC:CH],
                                                in1=xnb[:, :].rearrange("p (a t) -> p a t", a=EH),
                                                op=ALU.mult)
                                for chx in range(EH):
                                    i = v.tensor_scalar(out=yb[:, chx * CH + HC:(chx + 1) * CH],
                                                        in0=xnb[:, chx * HC:(chx + 1) * HC],
                                                        scalar1=D_sb[:, chx:chx + 1],
                                                        scalar2=None, op0=ALU.mult)
                                sem.inc("vv", i)
                            st["dtx_done"] = sem.val("vv")
                            st["xnb_free"] = ("vv", st["dtx_done"])

                            @block.gpsimd
                            def _(gp, cc=cc, ct0=ct0, gs=gs):
                                sem.wait(gp, "gpsimd", "vv", sub_done[gs - 1])
                                for pq in range(4):
                                    gp.dma_start(
                                        out=Bball[pq * 32:(pq + 1) * 32, :].rearrange("p (n t) -> p n t", n=NST),
                                        in_=bc_d[0:NST, ct0 + HC:ct0 + CH].unsqueeze(0).broadcast_to([32, NST, HC])
                                    ).then_inc(sem.h["ldg"], 16)
                                    sem.cnt["ldg"] += 16
                                    i = gp.dma_start(
                                        out=Cball[pq * 32:(pq + 1) * 32, :].rearrange("p (n t) -> p n t", n=NST),
                                        in_=bc_d[NST:32, ct0 + HC:ct0 + CH].unsqueeze(0).broadcast_to([32, NST, HC]))
                                    sem.inc("ldg", i, 16)
                            bc_ld[1] = sem.val("ldg")

                        @block.scalar
                        def _(sc, gs=gs, bs_=bs_, ts0=ts0, cc=cc):
                            sem.wait(sc, "scalar", "lds", c_ld)
                            sem.wait(sc, "scalar", "ldg", abc_ld)
                            if gs >= 2:
                                sem.wait(sc, "scalar", "vv", sub_done[gs - 2])
                            d3 = dl_c[:, :].rearrange("p (a t) -> p a t", a=EH)[:, :, ts0:ts0 + TS]
                            for n in range(NST):
                                i = sc.activation(
                                    out=dA[bs_][:, n * EH * TS:(n + 1) * EH * TS]
                                        .rearrange("p (a t) -> p a t", a=EH),
                                    in_=d3, func=AF.Exp, scale=A_bc[:, n:n + 1])
                            sem.inc("ev", i)
                        exp_done = sem.val("ev")

                        @block.vector
                        def _(v, gs=gs, bs_=bs_, ts0=ts0, hloc=hloc, s=s):
                            sem.wait(v, "vector", "ldg", bc_ld[1] if ts0 >= HC else bc_ld[0])
                            sem.wait(v, "vector", "ev", exp_done)
                            d4 = dtx[:, :].rearrange("p (a t) -> p a t", a=EH)[:, :, hloc:hloc + TS]
                            u4 = ub[:, :].rearrange("p (n a t) -> p n a t", n=NST, a=EH)
                            v.tensor_tensor(
                                out=u4,
                                in0=d4.unsqueeze(1).broadcast_to([P, NST, EH, TS]),
                                in1=Bball[:, :].rearrange("p (n t) -> p n t", n=NST)[:, :, hloc:hloc + TS]
                                    .unsqueeze(2).broadcast_to([P, NST, EH, TS]),
                                op=ALU.mult)
                            dA3 = dA[bs_][:, :].rearrange("p (g t) -> p g t", g=G)
                            v.tensor_copy(out=dA0[:, :].unsqueeze(2), in_=dA3[:, :, 0:1])
                            v.tensor_mul(tmpg[:, :], dA0[:, :], carry[:, :])
                            u3 = ub[:, :].rearrange("p (g t) -> p g t", g=G)
                            v.tensor_tensor(out=u3[:, :, 0:1], in0=u3[:, :, 0:1],
                                            in1=tmpg[:, :].unsqueeze(2), op=ALU.add)
                            v.memset(dA3[:, :, 0:1], 0.0)
                            v.tensor_tensor_scan(out=hb[:, :], data0=dA[bs_][:, :],
                                                 data1=ub[:, :], initial=0.0,
                                                 op0=ALU.mult, op1=ALU.add)
                            h3 = hb[:, :].rearrange("p (g t) -> p g t", g=G)
                            v.tensor_copy(out=carry[:, :].unsqueeze(2), in_=h3[:, :, TS - 1:TS])
                            v.tensor_tensor(
                                out=u4, in0=hb[:, :].rearrange("p (n a t) -> p n a t", n=NST, a=EH),
                                in1=Cball[:, :].rearrange("p (n t) -> p n t", n=NST)[:, :, hloc:hloc + TS]
                                    .unsqueeze(2).broadcast_to([P, NST, EH, TS]),
                                op=ALU.mult)
                            half = G * TS // 2
                            v.tensor_tensor(out=hb[:, 0:half], in0=ub[:, 0:half],
                                            in1=ub[:, half:2 * half], op=ALU.add)
                            q = half // 2
                            v.tensor_tensor(out=hb[:, half:half + q], in0=hb[:, 0:q],
                                            in1=hb[:, q:2 * q], op=ALU.add)
                            e8 = q // 2
                            v.tensor_tensor(out=hb[:, half + q:half + q + e8],
                                            in0=hb[:, half:half + e8],
                                            in1=hb[:, half + e8:half + 2 * e8], op=ALU.add)
                            e16 = e8 // 2
                            v.tensor_tensor(out=l4f[:, :], in0=hb[:, half + q:half + q + e16],
                                            in1=hb[:, half + q + e16:half + q + 2 * e16], op=ALU.add)
                            yv = yb[:, :].rearrange("p (a t) -> p a t", a=EH)[:, :, ts0:ts0 + TS]
                            i = v.tensor_tensor(
                                out=yv, in0=yv,
                                in1=l4f[:, :].rearrange("p (a t) -> p a t", a=EH),
                                op=ALU.add)
                            sem.inc("vv", i)
                        sub_done.append(sem.val("vv"))
                        if s == NSUB // 2 - 1:
                            emit_ffn_batch(ffn_jobs[:2])
                        elif s == NSUB - 1:
                            emit_ffn_batch(ffn_jobs[2:])

                    y_done.append(sub_done[-1])

                emit_wout(NCHUNK - 1)
                for tf_ in (14, 15):
                    emit_ffn_tile(1, tf_)
                    emit_ffn_fin()
            barrier()
        barrier()

        @block.sync
        def _(sync):
            sem.wait(sync, "sync", "sts", sem.val("sts"))

    return nc


def kernel(L0_attn_x, L0_attn_Wx, L0_attn_A_log, L0_attn_D, L0_attn_Wout,
           L0_ffn_w1, L0_ffn_b1, L0_ffn_w2, L0_ffn_b2):
    x = np.asarray(L0_attn_x, dtype=np.float32)
    Wx = np.asarray(L0_attn_Wx, dtype=np.float32)
    A_log = np.asarray(L0_attn_A_log, dtype=np.float32)
    D = np.asarray(L0_attn_D, dtype=np.float32)
    Wout = np.asarray(L0_attn_Wout, dtype=np.float32)
    w1 = np.asarray(L0_ffn_w1, dtype=np.float32)
    b1 = np.asarray(L0_ffn_b1, dtype=np.float32)
    w2 = np.asarray(L0_ffn_w2, dtype=np.float32)
    b2 = np.asarray(L0_ffn_b2, dtype=np.float32)

    wxt = np.ascontiguousarray(np.concatenate([Wx[32:], Wx[:32]], axis=0).T)
    wot = np.ascontiguousarray(Wout.T)
    w1c = np.ascontiguousarray(w1)
    w2c = np.ascontiguousarray(w2)
    alog = np.ascontiguousarray(A_log.reshape(1, 16))
    dvec = np.ascontiguousarray(D.reshape(8, 128).T)
    b1m = np.ascontiguousarray(b1.reshape(32, 128).T)
    b2m = np.ascontiguousarray(b2.reshape(8, 128).T)

    nc = build_program()
    in_maps = []
    for i in range(8):
        in_maps.append({
            "xin": np.ascontiguousarray(x[i].T),
            "wxt": wxt, "w1": w1c, "w2": w2c, "wot": wot,
            "alog": alog, "dvec": dvec, "b1m": b1m, "b2m": b2m,
        })
    res = run_bass_kernel_spmd(nc, in_maps, core_ids=list(range(8)))
    out = np.stack([np.asarray(r["yout"], dtype=np.float32).T for r in res.results], axis=0)
    return np.ascontiguousarray(out)


# version 16 (final)
# speedup vs baseline: 1.1963x; 1.0742x over previous
import numpy as np
from contextlib import ExitStack

import concourse.bass as bass
import concourse.mybir as mybir
from concourse.bass_utils import run_bass_kernel_spmd

F32 = mybir.dt.float32
BF16 = mybir.dt.bfloat16
AF = mybir.ActivationFunctionType
ALU = mybir.AluOpType

P = 128
T = 4096
EH = 8
H = 1024
NST = 16
J = 1056
FH = 4096
CH = 512
NCHUNK = T // CH
TS = 32
NSUB = CH // TS
TT = 512
NTILE = T // TT
G = 128
EPS = 1e-5


class Sems:

    def __init__(self, nc, stack, names):
        self.h = {n: stack.enter_context(nc.semaphore(n)) for n in names}
        self.cnt = {n: 0 for n in names}
        self.waited = {}

    def inc(self, name, instr, k=1):
        instr.then_inc(self.h[name], k)
        self.cnt[name] += k
        return self.cnt[name]

    def val(self, name):
        return self.cnt[name]

    def wait(self, eng, ename, name, target):
        if target is None or target <= 0:
            return
        key = (ename, name)
        if self.waited.get(key, 0) >= target:
            return
        eng.wait_ge(self.h[name], target)
        self.waited[key] = target


def build_program(phases=4, dbg=False):
    SCR = "ExternalOutput"
    nc = bass.Bass("TRN2", target_bir_lowering=False, debug=False)

    xin = nc.dram_tensor("xin", [H, T], F32, kind="ExternalInput").ap()
    wxt = nc.dram_tensor("wxt", [H, J], F32, kind="ExternalInput").ap()
    w1 = nc.dram_tensor("w1", [H, FH], F32, kind="ExternalInput").ap()
    w2 = nc.dram_tensor("w2", [FH, H], F32, kind="ExternalInput").ap()
    wot = nc.dram_tensor("wot", [H, H], F32, kind="ExternalInput").ap()
    alog = nc.dram_tensor("alog", [1, NST], F32, kind="ExternalInput").ap()
    dvec = nc.dram_tensor("dvec", [P, EH], F32, kind="ExternalInput").ap()
    b1m = nc.dram_tensor("b1m", [P, 32], F32, kind="ExternalInput").ap()
    b2m = nc.dram_tensor("b2m", [P, EH], F32, kind="ExternalInput").ap()
    ohb_h = nc.dram_tensor("ohb", [32, 32 * P], F32, kind="ExternalInput").ap()
    yout = nc.dram_tensor("yout", [H, T], F32, kind="ExternalOutput").ap()
    a_d = nc.dram_tensor("a_d", [1, NST], F32, kind=SCR).ap()
    ab_d = nc.dram_tensor("ab_d", [2, T], F32, kind=SCR).ap()
    bc_d = nc.dram_tensor("bc_d", [32, T], BF16, kind=SCR).ap()
    xn_d = nc.dram_tensor("xn_d", [H, T], BF16, kind=SCR).ap()
    delta_d = nc.dram_tensor("delta_d", [H, T], BF16, kind=SCR).ap()
    attn_d = nc.dram_tensor("attn_d", [H, T], BF16, kind=SCR).ap()
    acca_d = nc.dram_tensor("acca_d", [H, T], BF16, kind=SCR).ap()

    xin3 = xin.rearrange("(a p) t -> p a t", p=P)
    xn3 = xn_d.rearrange("(a p) t -> p a t", p=P)
    delta3 = delta_d.rearrange("(a p) t -> p a t", p=P)
    attn3 = attn_d.rearrange("(a p) t -> p a t", p=P)
    acca3 = acca_d.rearrange("(a p) t -> p a t", p=P)
    yout3 = yout.rearrange("(a p) t -> p a t", p=P)

    with ExitStack() as outer:
        sem = Sems(nc, outer, [
            "lds",
            "sts",
            "cst",
            "wot",
            "abc",
            "asp",
            "absp",
            "ab",
            "ld1a", "ld1b",
            "p1s",
            "chk",
            "bcc",
            "xnf", "res", "acb", "att",
            "attsp", "accsp",
            "w",
            "mm",
            "ev",
            "vv",
            "bs", "bg", "bt", "bc_", "bv",
        ])

        def sb(stack, name, shape, dt):
            return stack.enter_context(nc.sbuf_tensor(name, shape, dt))

        A_bc = sb(outer, "A_bc", [P, NST], F32)
        D_sb = sb(outer, "D_sb", [P, EH], F32)
        B1_sb = sb(outer, "B1_sb", [P, 32], F32)
        B2_sb = sb(outer, "B2_sb", [P, EH], F32)
        ones = sb(outer, "ones", [P, 1], F32)
        arow = sb(outer, "arow", [1, NST], F32)

        ps = [outer.enter_context(nc.psum_tensor(f"ps{i}", [P, 512], F32))
              for i in range(8)]
        bank_free = {i: None for i in range(8)}

        block = outer.enter_context(nc.Block())
        barrier_n = [0]

        def barrier():
            barrier_n[0] += 1
            k = barrier_n[0]
            pairs = [("sync", "bs"), ("gpsimd", "bg"), ("tensor", "bt"),
                     ("scalar", "bc_"), ("vector", "bv")]
            for ename, s in pairs:
                @getattr(block, ename)
                def _(eng, s=s):
                    eng.sem_inc(sem.h[s], 1)
                sem.cnt[s] += 1
            for ename, _s in pairs:
                @getattr(block, ename)
                def _(eng, ename=ename):
                    for _, s2 in pairs:
                        sem.wait(eng, ename, s2, k)

        with ExitStack() as s13:
            Wot = sb(s13, "Wot", [P, EH * H], BF16)

            @block.sync
            def _(sync):
                sync.dma_start(out=arow[:, :], in_=alog[:, :]).then_inc(sem.h["cst"], 16)
                sem.cnt["cst"] += 16
                sync.dma_start(out=D_sb[:, :], in_=dvec[:, :]).then_inc(sem.h["cst"], 16)
                sem.cnt["cst"] += 16
                sync.dma_start(out=B1_sb[:, :], in_=b1m[:, :]).then_inc(sem.h["cst"], 16)
                sem.cnt["cst"] += 16
                sync.dma_start(out=B2_sb[:, :], in_=b2m[:, :]).then_inc(sem.h["cst"], 16)
                sem.cnt["cst"] += 16
            ph0_lds = sem.val("cst")

            @block.gpsimd
            def _(gp):
                i = gp.dma_start(out=Wot[:, :].rearrange("p (a e) -> p a e", a=EH),
                                 in_=wot.rearrange("(a p) e -> p a e", p=P))
                sem.inc("wot", i, 16)
            wx_ld = None

            @block.scalar
            def _(sc):
                sem.wait(sc, "scalar", "cst", ph0_lds)
                i = sc.activation(out=arow[:, :], in_=arow[:, :], func=AF.Exp)
                sem.inc("ev", i)
            a_exp = sem.val("ev")

            @block.vector
            def _(v):
                v.memset(ones[:, :], 1.0)
                sem.wait(v, "vector", "ev", a_exp)
                i = v.tensor_scalar(out=arow[:, :], in0=arow[:, :], scalar1=-1.0,
                                    scalar2=None, op0=ALU.mult)
                sem.inc("vv", i)
            a_neg = sem.val("vv")

            @block.sync
            def _(sync):
                sem.wait(sync, "sync", "vv", a_neg)
                i = sync.dma_start(out=a_d[:, :], in_=arow[:, :])
                sem.inc("asp", i, 16)
            a_spill = sem.val("asp")

            @block.gpsimd
            def _(gp):
                sem.wait(gp, "gpsimd", "asp", a_spill)
                i = gp.dma_start(out=A_bc[:, :], in_=a_d[0:1, :].broadcast_to([P, NST]))
                sem.inc("abc", i, 16)
            abc_ld = sem.val("abc")

            with ExitStack() as s1:
                WxT = sb(s1, "WxT", [P, EH * J], BF16)
                xt = [sb(s1, f"xt{b}", [P, EH * TT], F32) for b in range(2)]
                xsq = sb(s1, "xsq", [P, EH * TT], F32)
                xn_t = [sb(s1, f"xnt{b}", [P, EH * TT], BF16) for b in range(2)]
                dl_t = [sb(s1, f"dlt{b}", [P, EH * TT], BF16) for b in range(2)]
                BCs = sb(s1, "BCs", [32, TT], BF16)
                a_bc = sb(s1, "a_bc", [P, TT], F32)
                b_bc = sb(s1, "b_bc", [P, TT], F32)
                rows = sb(s1, "rows", [1, 512 * 10], F32)

                def row(i):
                    return rows[:, i * 512:(i + 1) * 512]

                @block.gpsimd
                def _(gp):
                    i = gp.dma_start(out=WxT[:, :].rearrange("p (a j) -> p a j", a=EH),
                                     in_=wxt.rearrange("(a p) j -> p a j", p=P))
                    sem.inc("wot", i, 16)
                wxt_ld = sem.val("wot")

                x_loaded = []
                xt_free = [None, None]
                xn_free = [None, None]
                xsq_free = None
                abc_free = None
                ab_sp_prev = None
                bc_sp_prev = None
                bc_sp = []
                dl_free = [None, None]
                spill_cnt = []
                dl_spill = []
                dlb_evcnt = {}

                spsc = sb(s1, "spsc", [P, TT], F32)
                xn_done_t = {}

                def stage_a(tt):
                    nonlocal abc_free, ab_sp_prev
                    b = tt % 2
                    t0 = tt * TT

                    ldx = "ld1a" if b == 0 else "ld1b"

                    @block.sync
                    def _(sync, b=b, t0=t0, ldx=ldx):
                        if xt_free[b] is not None:
                            sem.wait(sync, "sync", *xt_free[b])
                        i = sync.dma_start(out=xt[b][:, :].rearrange("p (a t) -> p a t", a=EH),
                                           in_=xin3[:, :, t0:t0 + TT])
                        sem.inc(ldx, i, 16)
                    x_loaded.append((ldx, sem.val(ldx)))

                    @block.scalar
                    def _(sc, b=b):
                        sem.wait(sc, "scalar", *x_loaded[-1])
                        if xsq_free[0] is not None:
                            sem.wait(sc, "scalar", *xsq_free[0])
                        i = sc.activation(out=xsq[:, :], in_=xt[b][:, :], func=AF.Square)
                        sem.inc("ev", i)
                    xsq_done = sem.val("ev")

                    @block.tensor
                    def _(te, b=b):
                        sem.wait(te, "tensor", *x_loaded[-1])
                        sem.wait(te, "tensor", "ev", xsq_done)
                        if bank_free[4] is not None:
                            sem.wait(te, "tensor", *bank_free[4])
                        if bank_free[5] is not None:
                            sem.wait(te, "tensor", *bank_free[5])
                        for kc in range(EH):
                            te.matmul(out=ps[4][0:1, 0:TT], lhsT=ones[:, :],
                                      rhs=xt[b][:, kc * TT:(kc + 1) * TT],
                                      start=(kc == 0), stop=(kc == EH - 1))
                        for kc in range(EH):
                            i = te.matmul(out=ps[5][0:1, 0:TT], lhsT=ones[:, :],
                                          rhs=xsq[:, kc * TT:(kc + 1) * TT],
                                          start=(kc == 0), stop=(kc == EH - 1))
                        sem.inc("mm", i)
                    stats_done = sem.val("mm")
                    xsq_free[0] = ("mm", stats_done)

                    @block.vector
                    def _(v):
                        sem.wait(v, "vector", "mm", stats_done)
                        if ab_sp_prev is not None:
                            sem.wait(v, "vector", "absp", ab_sp_prev)
                        v.tensor_scalar(out=row(0), in0=ps[4][0:1, 0:TT],
                                        scalar1=1.0 / H, scalar2=None, op0=ALU.mult)
                        v.tensor_scalar(out=row(1), in0=ps[5][0:1, 0:TT],
                                        scalar1=1.0 / H, scalar2=None, op0=ALU.mult)
                        v.tensor_mul(row(2), row(0), row(0))
                        v.tensor_sub(row(3), row(1), row(2))
                        i = v.tensor_scalar(out=row(4), in0=row(3), scalar1=EPS,
                                            scalar2=None, op0=ALU.add)
                        sem.inc("vv", i)
                    veps_done = sem.val("vv")
                    bank_free[4] = ("vv", veps_done)
                    bank_free[5] = ("vv", veps_done)

                    @block.scalar
                    def _(sc):
                        sem.wait(sc, "scalar", "vv", veps_done)
                        i = sc.activation(out=row(5), in_=row(4), func=AF.Sqrt)
                        sem.inc("ev", i)
                    sq_done = sem.val("ev")

                    @block.vector
                    def _(v):
                        sem.wait(v, "vector", "ev", sq_done)
                        v.reciprocal(row(6), row(5))
                        v.tensor_mul(row(2), row(6), row(6))
                        v.tensor_mul(row(1), row(2), row(4))
                        v.tensor_scalar(out=row(2), in0=row(1), scalar1=-0.5,
                                        scalar2=1.5, op0=ALU.mult, op1=ALU.add)
                        v.tensor_mul(row(7), row(6), row(2))
                        v.tensor_mul(row(8), row(0), row(7))
                        i = v.tensor_scalar(out=row(8), in0=row(8), scalar1=-1.0,
                                            scalar2=None, op0=ALU.mult)
                        sem.inc("vv", i)
                    ab_done = sem.val("vv")

                    @block.sync
                    def _(sync, t0=t0):
                        sem.wait(sync, "sync", "vv", ab_done)
                        sync.dma_start(out=ab_d[0:1, t0:t0 + TT], in_=row(7)).then_inc(sem.h["absp"], 16)
                        sem.cnt["absp"] += 16
                        i = sync.dma_start(out=ab_d[1:2, t0:t0 + TT], in_=row(8))
                        sem.inc("absp", i, 16)
                    ab_sp = sem.val("absp")
                    ab_sp_prev = ab_sp

                    @block.gpsimd
                    def _(gp, t0=t0):
                        sem.wait(gp, "gpsimd", "absp", ab_sp)
                        if abc_free is not None:
                            sem.wait(gp, "gpsimd", *abc_free)
                        gp.dma_start(out=a_bc[:, :],
                                     in_=ab_d[0:1, t0:t0 + TT].broadcast_to([P, TT])).then_inc(sem.h["ab"], 16)
                        sem.cnt["ab"] += 16
                        i = gp.dma_start(out=b_bc[:, :],
                                         in_=ab_d[1:2, t0:t0 + TT].broadcast_to([P, TT]))
                        sem.inc("ab", i, 16)
                    abc_ld2 = sem.val("ab")

                    @block.vector
                    def _(v, b=b):
                        sem.wait(v, "vector", "ab", abc_ld2)
                        if xn_free[b] is not None:
                            for cond in xn_free[b]:
                                sem.wait(v, "vector", *cond)
                        v.tensor_tensor(
                            out=xt[b][:, :].rearrange("p (a t) -> p a t", a=EH),
                            in0=xt[b][:, :].rearrange("p (a t) -> p a t", a=EH),
                            in1=a_bc[:, :].unsqueeze(1).broadcast_to([P, EH, TT]),
                            op=ALU.mult)
                        i = v.tensor_tensor(
                            out=xn_t[b][:, :].rearrange("p (a t) -> p a t", a=EH),
                            in0=xt[b][:, :].rearrange("p (a t) -> p a t", a=EH),
                            in1=b_bc[:, :].unsqueeze(1).broadcast_to([P, EH, TT]),
                            op=ALU.add)
                        sem.inc("vv", i)
                    xn_done_t[tt] = sem.val("vv")
                    xt_free[b] = ("vv", xn_done_t[tt])
                    abc_free = ("vv", xn_done_t[tt])

                    @block.sync
                    def _(sync, b=b, t0=t0, tt=tt):
                        sem.wait(sync, "sync", "vv", xn_done_t[tt])
                        i = sync.dma_start(out=xn3[:, :, t0:t0 + TT],
                                           in_=xn_t[b][:, :].rearrange("p (a t) -> p a t", a=EH))
                        sem.inc("p1s", i, 16)
                    spill_cnt.append(sem.val("p1s"))

                def stage_b(tt):
                    nonlocal bc_sp_prev
                    b = tt % 2
                    t0 = tt * TT
                    ev_base2 = sem.val("ev")
                    for jb in [8] + list(range(8)):
                        bank = 2 if jb == 8 else jb % 2
                        mwid = 32 if jb == 8 else P

                        @block.tensor
                        def _(te, b=b, tt=tt, jb=jb, bank=bank, mwid=mwid):
                            sem.wait(te, "tensor", "vv", xn_done_t[tt])
                            sem.wait(te, "tensor", "wot", wxt_ld)
                            if jb in (8, 0, 1):
                                if bank_free[bank] is not None:
                                    sem.wait(te, "tensor", *bank_free[bank])
                            else:
                                sem.wait(te, "tensor", "ev", ev_base2 + jb)
                            for kc in range(EH):
                                i = te.matmul(
                                    out=ps[bank][0:mwid, 0:TT],
                                    lhsT=WxT[:, kc * J + jb * P:kc * J + jb * P + mwid],
                                    rhs=xn_t[b][:, kc * TT:(kc + 1) * TT],
                                    start=(kc == 0), stop=(kc == EH - 1))
                            sem.inc("mm", i)
                            dlb_evcnt[(tt, jb)] = sem.val("mm")
                    wx_done = sem.val("mm")
                    xn_free[b] = [("p1s", spill_cnt[tt]), ("mm", wx_done)]

                    @block.scalar
                    def _(sc, b=b, tt=tt):
                        sem.wait(sc, "scalar", "mm", dlb_evcnt[(tt, 8)])
                        if bc_sp_prev is not None:
                            sem.wait(sc, "scalar", "p1s", bc_sp_prev)
                        i = sc.activation(out=BCs[:, :], in_=ps[2][0:32, 0:TT],
                                          func=AF.Copy)
                        sem.inc("ev", i)
                        if dl_free[b] is not None:
                            sem.wait(sc, "scalar", *dl_free[b])
                        for jb in range(8):
                            bank = jb % 2
                            sem.wait(sc, "scalar", "mm", dlb_evcnt[(tt, jb)])
                            sc.activation(out=spsc[:, :], in_=ps[bank][:, 0:TT],
                                          func=AF.Exp)
                            i = sc.activation(out=dl_t[b][:, jb * TT:(jb + 1) * TT],
                                              in_=spsc[:, :], func=AF.Ln, bias=1.0)
                            sem.inc("ev", i)
                    assert sem.val("ev") == ev_base2 + 9
                    bank_free[2] = ("ev", ev_base2 + 1)
                    bank_free[0] = ("ev", ev_base2 + 8)
                    bank_free[1] = ("ev", ev_base2 + 9)
                    soft_done = sem.val("ev")

                    @block.sync
                    def _(sync, b=b, t0=t0, tt=tt):
                        sem.wait(sync, "sync", "ev", soft_done)
                        sync.dma_start(out=bc_d[:, t0:t0 + TT],
                                       in_=BCs[:, :]).then_inc(sem.h["p1s"], 16)
                        sem.cnt["p1s"] += 16
                        i = sync.dma_start(out=delta3[:, :, t0:t0 + TT],
                                           in_=dl_t[b][:, :].rearrange("p (a t) -> p a t", a=EH))
                        sem.inc("p1s", i, 16)
                    dl_spill.append(sem.val("p1s"))
                    bc_sp.append(sem.val("p1s"))
                    bc_sp_prev = sem.val("p1s")
                    dl_free[b] = ("p1s", dl_spill[-1])

                xsq_free = [None]
                stage_a(0)
                stage_a(1)
                for tt in range(NTILE):
                    stage_b(tt)
                    if tt + 2 < NTILE:
                        stage_a(tt + 2)

            barrier()

            if phases < 3:
                @block.sync
                def _(sync):
                    sem.wait(sync, "sync", "sts", sem.val("sts"))
                return nc
            with ExitStack() as s3:
                HC = CH // 2
                xnb = sb(s3, "xnb", [P, EH * HC], BF16)
                dl_c = sb(s3, "dlc", [P, EH * CH], BF16)
                dtx = sb(s3, "dtx", [P, EH * HC], BF16)
                dA = [sb(s3, f"dA{b}", [P, G * TS], BF16) for b in range(2)]
                ub = sb(s3, "ub", [P, G * TS], BF16)
                hb = sb(s3, "hb", [P, G * TS], BF16)
                l4f = sb(s3, "l4f", [P, EH * TS], F32)
                yb = sb(s3, "yb", [P, EH * CH], BF16)
                att_t = sb(s3, "att_t", [P, EH * CH], BF16)
                carry = sb(s3, "carry", [P, G], F32)
                dA0 = sb(s3, "dA0v", [P, G], F32)
                tmpg = sb(s3, "tmpg", [P, G], F32)
                Bball = sb(s3, "Bball", [P, NST * HC], BF16)
                Cball = sb(s3, "Cball", [P, NST * HC], BF16)
                BCc = sb(s3, "BCc", [32, CH], BF16)
                OHB = sb(s3, "OHB", [32, 32 * P], BF16)
                HHALF = FH // 2
                FT = 256
                NFT = T // FT
                xnf = sb(s3, "xnf", [P, EH * FT], BF16)
                w1h = sb(s3, "w1h", [P, EH * HHALF], BF16)
                w2h = sb(s3, "w2h", [P, 16 * H], BF16)
                resf = sb(s3, "resf", [P, EH * FT], F32)
                attf = sb(s3, "attf", [P, EH * FT], BF16)
                acb = sb(s3, "acb", [P, EH * FT], BF16)
                hf = sb(s3, "hf", [P, 16 * FT], BF16)
                outf = sb(s3, "outf", [P, EH * FT], F32)

                @block.vector
                def _(v):
                    v.memset(carry[:, :], 0.0)

                @block.gpsimd
                def _(gp):
                    i = gp.dma_start(out=OHB[:, :], in_=ohb_h[:, :])
                    sem.inc("wot", i, 16)
                ohb_done = sem.val("wot")
                wot_total = sem.val("wot")

                @block.gpsimd
                def _(gp):
                    gp.dma_start(
                        out=w1h[:, :].rearrange("p (a h) -> p a h", a=EH),
                        in_=w1.rearrange("(a p) h -> p a h", p=P)[:, :, 0:HHALF]
                    ).then_inc(sem.h["w"], 16)
                    sem.cnt["w"] += 16
                    i = gp.dma_start(
                        out=w2h[:, :].rearrange("p (a e) -> p a e", a=16),
                        in_=w2.rearrange("(a p) e -> p a e", p=P)[:, 0:16, :])
                    sem.inc("w", i, 16)
                w_ld = sem.val("w")

                p1s_total = sem.val("p1s")
                sub_done = []
                y_done = []
                att_sp = []
                acca_sp = {}
                dlb_evcnt = {}
                st = {"xnb_free": None, "dl_free": None, "yb_free": None,
                      "att_free": None, "resf_free": None, "attf_free": None,
                      "outf_free": None, "hf_free": None, "bc_free": None,
                      "passA_mm": None, "dtx_done": 0, "xnf_free": None,
                      "wb_ld": None, "acb_free": None}
                pend_fin = []

                def emit_wout(cc):
                    evw_base = sem.val("ev")
                    for eb in range(EH):
                        bank = 2 + eb % 2

                        @block.tensor
                        def _(te, cc=cc, eb=eb, bank=bank):
                            sem.wait(te, "tensor", "vv", y_done[cc])
                            sem.wait(te, "tensor", "wot", wot_total)
                            if eb < 2:
                                if bank_free[bank] is not None:
                                    sem.wait(te, "tensor", *bank_free[bank])
                            else:
                                sem.wait(te, "tensor", "ev", evw_base + eb - 1)
                            for kc in range(EH):
                                i = te.matmul(
                                    out=ps[bank][:, 0:CH],
                                    lhsT=Wot[:, kc * H + eb * P:kc * H + (eb + 1) * P],
                                    rhs=yb[:, kc * CH:(kc + 1) * CH],
                                    start=(kc == 0), stop=(kc == EH - 1))
                            sem.inc("mm", i)
                            dlb_evcnt[("wo", cc, eb)] = sem.val("mm")

                    @block.scalar
                    def _(sc, cc=cc):
                        if att_sp:
                            sem.wait(sc, "scalar", "attsp", att_sp[-1])
                        for eb in range(EH):
                            bank = 2 + eb % 2
                            sem.wait(sc, "scalar", "mm", dlb_evcnt[("wo", cc, eb)])
                            i = sc.activation(out=att_t[:, eb * CH:(eb + 1) * CH],
                                              in_=ps[bank][:, 0:CH], func=AF.Copy)
                            sem.inc("ev", i)
                    assert sem.val("ev") == evw_base + EH
                    bank_free[2] = ("ev", evw_base + EH - 1)
                    bank_free[3] = ("ev", evw_base + EH)
                    wev = sem.val("ev")

                    @block.sync
                    def _(sync, cc=cc, wev=wev):
                        sem.wait(sync, "sync", "ev", wev)
                        i = sync.dma_start(out=attn3[:, :, cc * CH:(cc + 1) * CH],
                                           in_=att_t[:, :].rearrange("p (a t) -> p a t", a=EH))
                        sem.inc("attsp", i, 16)
                    att_sp.append(sem.val("attsp"))
                    st["yb_free"] = ("mm", dlb_evcnt[("wo", cc, EH - 1)])

                def emit_ffn_tile(pa, tf):
                    t0 = tf * FT

                    @block.sync
                    def _(sync, pa=pa, tf=tf, t0=t0):
                        if st["xnf_free"] is not None:
                            sem.wait(sync, "sync", *st["xnf_free"])
                        sem.wait(sync, "sync", "p1s", p1s_total)
                        i = sync.dma_start(out=xnf[:, :].rearrange("p (a t) -> p a t", a=EH),
                                           in_=xn3[:, :, t0:t0 + FT])
                        sem.inc("xnf", i, 16)
                        if pa == 1:
                            if st["resf_free"] is not None:
                                sem.wait(sync, "sync", *st["resf_free"])
                            sync.dma_start(out=resf[:, :].rearrange("p (a t) -> p a t", a=EH),
                                           in_=xin3[:, :, t0:t0 + FT]).then_inc(sem.h["res"], 16)
                            sem.cnt["res"] += 16
                            if st["acb_free"] is not None:
                                sem.wait(sync, "sync", *st["acb_free"])
                            sem.wait(sync, "sync", "accsp", acca_sp[tf])
                            sync.dma_start(out=acb[:, :].rearrange("p (a t) -> p a t", a=EH),
                                           in_=acca3[:, :, t0:t0 + FT]).then_inc(sem.h["acb"], 16)
                            sem.cnt["acb"] += 16
                            if st["attf_free"] is not None:
                                sem.wait(sync, "sync", *st["attf_free"])
                            sem.wait(sync, "sync", "attsp", att_sp[tf // 2])
                            i = sync.dma_start(out=attf[:, :].rearrange("p (a t) -> p a t", a=EH),
                                               in_=attn3[:, :, t0:t0 + FT])
                            sem.inc("att", i, 16)
                    f_ld = (sem.val("xnf"), sem.val("res"), sem.val("acb"), sem.val("att"))

                    evt_base = sem.val("ev")
                    for hb_ in range(16):
                        bank = hb_ % 2

                        @block.tensor
                        def _(te, pa=pa, hb_=hb_, bank=bank):
                            sem.wait(te, "tensor", "xnf", f_ld[0])
                            sem.wait(te, "tensor", "w",
                                     w_ld if pa == 0 else st["wb_ld"])
                            if hb_ == 0 and st["hf_free"] is not None:
                                sem.wait(te, "tensor", *st["hf_free"])
                            if hb_ < 2:
                                if bank_free[bank] is not None:
                                    sem.wait(te, "tensor", *bank_free[bank])
                            else:
                                sem.wait(te, "tensor", "ev", evt_base + hb_ - 1)
                            for kc in range(EH):
                                i = te.matmul(
                                    out=ps[bank][:, 0:FT],
                                    lhsT=w1h[:, kc * HHALF + hb_ * P:kc * HHALF + (hb_ + 1) * P],
                                    rhs=xnf[:, kc * FT:(kc + 1) * FT],
                                    start=(kc == 0), stop=(kc == EH - 1))
                            sem.inc("mm", i)
                            dlb_evcnt[("f1", pa, tf, hb_)] = sem.val("mm")

                    @block.scalar
                    def _(sc, pa=pa, tf=tf):
                        for hb_ in range(16):
                            bank = hb_ % 2
                            sem.wait(sc, "scalar", "mm", dlb_evcnt[("f1", pa, tf, hb_)])
                            i = sc.activation(out=hf[:, hb_ * FT:(hb_ + 1) * FT],
                                              in_=ps[bank][:, 0:FT], func=AF.Gelu,
                                              bias=B1_sb[:, pa * 16 + hb_:pa * 16 + hb_ + 1])
                            sem.inc("ev", i)
                    assert sem.val("ev") == evt_base + 16
                    bank_free[0] = ("ev", evt_base + 15)
                    bank_free[1] = ("ev", evt_base + 16)
                    gelu_done = sem.val("ev")

                    evq_base = sem.val("ev")
                    for eb in range(EH):
                        bank = 4 + eb % 2

                        @block.tensor
                        def _(te, pa=pa, eb=eb, bank=bank):
                            sem.wait(te, "tensor", "ev", gelu_done)
                            if eb < 2:
                                if bank_free[bank] is not None:
                                    sem.wait(te, "tensor", *bank_free[bank])
                            else:
                                sem.wait(te, "tensor", "ev", evq_base + eb - 1)
                            for kc in range(16):
                                i = te.matmul(
                                    out=ps[bank][:, 0:FT],
                                    lhsT=w2h[:, kc * H + eb * P:kc * H + (eb + 1) * P],
                                    rhs=hf[:, kc * FT:(kc + 1) * FT],
                                    start=(kc == 0), stop=(kc == 15))
                            sem.inc("mm", i)
                            dlb_evcnt[("f2", pa, tf, eb)] = sem.val("mm")
                    st["hf_free"] = ("mm", dlb_evcnt[("f2", pa, tf, EH - 1)])
                    st["xnf_free"] = ("mm", dlb_evcnt[("f1", pa, tf, 15)])

                    @block.scalar
                    def _(sc, pa=pa, tf=tf):
                        if pa == 0 and st["attf_free"] is not None:
                            sem.wait(sc, "scalar", *st["attf_free"])
                        if pa == 1 and st["outf_free"] is not None:
                            sem.wait(sc, "scalar", *st["outf_free"])
                        for eb in range(EH):
                            bank = 4 + eb % 2
                            sem.wait(sc, "scalar", "mm", dlb_evcnt[("f2", pa, tf, eb)])
                            if pa == 0:
                                i = sc.activation(out=attf[:, eb * FT:(eb + 1) * FT],
                                                  in_=ps[bank][:, 0:FT], func=AF.Copy)
                            else:
                                i = sc.activation(out=outf[:, eb * FT:(eb + 1) * FT],
                                                  in_=ps[bank][:, 0:FT], func=AF.Identity,
                                                  bias=B2_sb[:, eb:eb + 1])
                            sem.inc("ev", i)
                    assert sem.val("ev") == evq_base + EH
                    bank_free[4] = ("ev", evq_base + EH - 1)
                    bank_free[5] = ("ev", evq_base + EH)
                    fev = sem.val("ev")

                    if pa == 0:
                        @block.sync
                        def _(sync, tf=tf, t0=t0, fev=fev):
                            sem.wait(sync, "sync", "ev", fev)
                            i = sync.dma_start(out=acca3[:, :, t0:t0 + FT],
                                               in_=attf[:, :].rearrange("p (a t) -> p a t", a=EH))
                            sem.inc("accsp", i, 16)
                        acca_sp[tf] = sem.val("accsp")
                        st["attf_free"] = ("accsp", acca_sp[tf])
                        st["passA_mm"] = st["hf_free"]
                        pend_fin.append(None)
                    else:
                        pend_fin.append((tf, t0, fev, f_ld))

                def emit_ffn_fin():
                    while pend_fin:
                        item = pend_fin.pop(0)
                        if item is None:
                            continue
                        tf, t0, fev, f_ld = item

                        @block.vector
                        def _(v, fev=fev, f_ld=f_ld):
                            sem.wait(v, "vector", "ev", fev)
                            sem.wait(v, "vector", "res", f_ld[1])
                            sem.wait(v, "vector", "acb", f_ld[2])
                            sem.wait(v, "vector", "att", f_ld[3])
                            v.tensor_tensor(out=outf[:, :], in0=outf[:, :],
                                            in1=resf[:, :], op=ALU.add)
                            v.tensor_tensor(out=outf[:, :], in0=outf[:, :],
                                            in1=acb[:, :], op=ALU.add)
                            i = v.tensor_tensor(out=outf[:, :], in0=outf[:, :],
                                                in1=attf[:, :], op=ALU.add)
                            sem.inc("vv", i)
                        fin = sem.val("vv")
                        st["resf_free"] = ("vv", fin)
                        st["attf_free"] = ("vv", fin)
                        st["acb_free"] = ("vv", fin)

                        @block.sync
                        def _(sync, tf=tf, t0=t0, fin=fin):
                            sem.wait(sync, "sync", "vv", fin)
                            i = sync.dma_start(out=yout3[:, :, t0:t0 + FT],
                                               in_=outf[:, :].rearrange("p (a t) -> p a t", a=EH))
                            sem.inc("sts", i, 16)
                        st["outf_free"] = ("sts", sem.val("sts"))

                for cc in range(NCHUNK):
                    ct0 = cc * CH

                    @block.sync
                    def _(sync, cc=cc, ct0=ct0):
                        sem.wait(sync, "sync", "p1s", p1s_total)
                        if st["xnb_free"] is not None:
                            sem.wait(sync, "sync", *st["xnb_free"])
                        if cc >= 1:
                            sem.wait(sync, "sync", "vv", sub_done[cc * NSUB - 1])
                        sync.dma_start(out=xnb[:, :].rearrange("p (a t) -> p a t", a=EH),
                                       in_=xn3[:, :, ct0:ct0 + HC]).then_inc(sem.h["chk"], 16)
                        sem.cnt["chk"] += 16
                        i = sync.dma_start(out=dl_c[:, :].rearrange("p (a t) -> p a t", a=EH),
                                           in_=delta3[:, :, ct0:ct0 + CH])
                        sem.inc("chk", i, 16)
                    c_ld = sem.val("chk")

                    @block.sync
                    def _(sync, cc=cc, ct0=ct0):
                        sem.wait(sync, "sync", "p1s", p1s_total)
                        if cc >= 1:
                            sem.wait(sync, "sync", "vv", sub_done[cc * NSUB - 1])
                        i = sync.dma_start(out=BCc[:, :], in_=bc_d[:, ct0:ct0 + CH])
                        sem.inc("bcc", i, 16)
                    bcc_ld = sem.val("bcc")

                    def emit_bc_rep(half):
                        hb0 = half * HC
                        evb = sem.val("ev")
                        for j in range(32):
                            bank = 6 + j % 2

                            @block.tensor
                            def _(te, j=j, bank=bank, hb0=hb0, evb=evb):
                                sem.wait(te, "tensor", "bcc", bcc_ld)
                                sem.wait(te, "tensor", "wot", wot_total)
                                if half == 0 and cc >= 1:
                                    sem.wait(te, "tensor", "vv", sub_done[cc * NSUB - 1])
                                elif half == 1:
                                    sem.wait(te, "tensor", "vv", sub_done[cc * NSUB + NSUB // 2 - 1])
                                if j < 2:
                                    if bank_free[bank] is not None:
                                        sem.wait(te, "tensor", *bank_free[bank])
                                else:
                                    sem.wait(te, "tensor", "ev", evb + j - 1)
                                i = te.matmul(out=ps[bank][:, 0:HC],
                                              lhsT=OHB[:, j * P:(j + 1) * P],
                                              rhs=BCc[:, hb0:hb0 + HC],
                                              start=True, stop=True)
                                sem.inc("mm", i)
                                dlb_evcnt[("bc", cc, half, j)] = sem.val("mm")

                        @block.scalar
                        def _(sc, hb0=hb0):
                            for j in range(32):
                                bank = 6 + j % 2
                                sem.wait(sc, "scalar", "mm", dlb_evcnt[("bc", cc, half, j)])
                                dst = Bball if j < 16 else Cball
                                n = j % 16
                                i = sc.activation(out=dst[:, n * HC:(n + 1) * HC],
                                                  in_=ps[bank][:, 0:HC], func=AF.Copy)
                                sem.inc("ev", i)
                        assert sem.val("ev") == evb + 32
                        bank_free[6] = ("ev", evb + 31)
                        bank_free[7] = ("ev", evb + 32)
                        return sem.val("ev")

                    bc_ld = [emit_bc_rep(0), None]

                    if cc >= 1:
                        emit_wout(cc - 1)

                    @block.vector
                    def _(v, cc=cc):
                        sem.wait(v, "vector", "chk", c_ld)
                        if st["yb_free"] is not None:
                            sem.wait(v, "vector", *st["yb_free"])
                        v.tensor_tensor(out=dtx[:, :],
                                        in0=dl_c[:, :].rearrange("p (a t) -> p a t", a=EH)[:, :, 0:HC],
                                        in1=xnb[:, :].rearrange("p (a t) -> p a t", a=EH),
                                        op=ALU.mult)
                        for chx in range(EH):
                            i = v.tensor_scalar(out=yb[:, chx * CH:chx * CH + HC],
                                                in0=xnb[:, chx * HC:(chx + 1) * HC],
                                                scalar1=D_sb[:, chx:chx + 1],
                                                scalar2=None, op0=ALU.mult)
                        sem.inc("vv", i)
                    st["dtx_done"] = sem.val("vv")
                    st["xnb_free"] = ("vv", st["dtx_done"])

                    if cc <= 3:
                        ffn_jobs = [(0, 4 * cc + q) for q in range(4)]
                    elif cc <= 6:
                        ffn_jobs = [(1, 4 * (cc - 4) + q) for q in range(4)]
                    else:
                        ffn_jobs = [(1, 12), (1, 13)]

                    def emit_ffn_batch(jobs):
                        for pa_, tf_ in jobs:
                            emit_ffn_tile(pa_, tf_)
                            emit_ffn_fin()
                        if cc == 3 and jobs and jobs[-1][1] == 15:
                            @block.gpsimd
                            def _(gp):
                                sem.wait(gp, "gpsimd", *st["passA_mm"])
                                gp.dma_start(
                                    out=w1h[:, :].rearrange("p (a h) -> p a h", a=EH),
                                    in_=w1.rearrange("(a p) h -> p a h", p=P)[:, :, HHALF:FH]
                                ).then_inc(sem.h["w"], 16)
                                sem.cnt["w"] += 16
                                i = gp.dma_start(
                                    out=w2h[:, :].rearrange("p (a e) -> p a e", a=16),
                                    in_=w2.rearrange("(a p) e -> p a e", p=P)[:, 16:32, :])
                                sem.inc("w", i, 16)
                            st["wb_ld"] = sem.val("w")

                    for s in range(NSUB):
                        gs = cc * NSUB + s
                        bs_ = s % 2
                        ts0 = s * TS
                        hloc = ts0 % HC

                        if s == NSUB // 2:
                            @block.sync
                            def _(sync, cc=cc, ct0=ct0):
                                sem.wait(sync, "sync", "vv", st["dtx_done"])
                                i = sync.dma_start(
                                    out=xnb[:, :].rearrange("p (a t) -> p a t", a=EH),
                                    in_=xn3[:, :, ct0 + HC:ct0 + CH])
                                sem.inc("chk", i, 16)
                            h2_ld = sem.val("chk")

                            @block.vector
                            def _(v, cc=cc):
                                sem.wait(v, "vector", "chk", h2_ld)
                                v.tensor_tensor(out=dtx[:, :],
                                                in0=dl_c[:, :].rearrange("p (a t) -> p a t", a=EH)[:, :, HC:CH],
                                                in1=xnb[:, :].rearrange("p (a t) -> p a t", a=EH),
                                                op=ALU.mult)
                                for chx in range(EH):
                                    i = v.tensor_scalar(out=yb[:, chx * CH + HC:(chx + 1) * CH],
                                                        in0=xnb[:, chx * HC:(chx + 1) * HC],
                                                        scalar1=D_sb[:, chx:chx + 1],
                                                        scalar2=None, op0=ALU.mult)
                                sem.inc("vv", i)
                            st["dtx_done"] = sem.val("vv")
                            st["xnb_free"] = ("vv", st["dtx_done"])

                            bc_ld[1] = emit_bc_rep(1)

                        @block.scalar
                        def _(sc, gs=gs, bs_=bs_, ts0=ts0, cc=cc):
                            sem.wait(sc, "scalar", "chk", c_ld)
                            sem.wait(sc, "scalar", "abc", abc_ld)
                            if gs >= 2:
                                sem.wait(sc, "scalar", "vv", sub_done[gs - 2])
                            d3 = dl_c[:, :].rearrange("p (a t) -> p a t", a=EH)[:, :, ts0:ts0 + TS]
                            for n in range(NST):
                                i = sc.activation(
                                    out=dA[bs_][:, n * EH * TS:(n + 1) * EH * TS]
                                        .rearrange("p (a t) -> p a t", a=EH),
                                    in_=d3, func=AF.Exp, scale=A_bc[:, n:n + 1])
                            sem.inc("ev", i)
                        exp_done = sem.val("ev")

                        @block.vector
                        def _(v, gs=gs, bs_=bs_, ts0=ts0, hloc=hloc, s=s):
                            sem.wait(v, "vector", "ev", bc_ld[1] if ts0 >= HC else bc_ld[0])
                            sem.wait(v, "vector", "ev", exp_done)
                            d4 = dtx[:, :].rearrange("p (a t) -> p a t", a=EH)[:, :, hloc:hloc + TS]
                            u4 = ub[:, :].rearrange("p (n a t) -> p n a t", n=NST, a=EH)
                            v.tensor_tensor(
                                out=u4,
                                in0=d4.unsqueeze(1).broadcast_to([P, NST, EH, TS]),
                                in1=Bball[:, :].rearrange("p (n t) -> p n t", n=NST)[:, :, hloc:hloc + TS]
                                    .unsqueeze(2).broadcast_to([P, NST, EH, TS]),
                                op=ALU.mult)
                            dA3 = dA[bs_][:, :].rearrange("p (g t) -> p g t", g=G)
                            v.tensor_copy(out=dA0[:, :].unsqueeze(2), in_=dA3[:, :, 0:1])
                            v.tensor_mul(tmpg[:, :], dA0[:, :], carry[:, :])
                            u3 = ub[:, :].rearrange("p (g t) -> p g t", g=G)
                            v.tensor_tensor(out=u3[:, :, 0:1], in0=u3[:, :, 0:1],
                                            in1=tmpg[:, :].unsqueeze(2), op=ALU.add)
                            v.memset(dA3[:, :, 0:1], 0.0)
                            v.tensor_tensor_scan(out=hb[:, :], data0=dA[bs_][:, :],
                                                 data1=ub[:, :], initial=0.0,
                                                 op0=ALU.mult, op1=ALU.add)
                            h3 = hb[:, :].rearrange("p (g t) -> p g t", g=G)
                            v.tensor_copy(out=carry[:, :].unsqueeze(2), in_=h3[:, :, TS - 1:TS])
                            v.tensor_tensor(
                                out=u4, in0=hb[:, :].rearrange("p (n a t) -> p n a t", n=NST, a=EH),
                                in1=Cball[:, :].rearrange("p (n t) -> p n t", n=NST)[:, :, hloc:hloc + TS]
                                    .unsqueeze(2).broadcast_to([P, NST, EH, TS]),
                                op=ALU.mult)
                            half = G * TS // 2
                            v.tensor_tensor(out=hb[:, 0:half], in0=ub[:, 0:half],
                                            in1=ub[:, half:2 * half], op=ALU.add)
                            q = half // 2
                            v.tensor_tensor(out=hb[:, half:half + q], in0=hb[:, 0:q],
                                            in1=hb[:, q:2 * q], op=ALU.add)
                            e8 = q // 2
                            v.tensor_tensor(out=hb[:, half + q:half + q + e8],
                                            in0=hb[:, half:half + e8],
                                            in1=hb[:, half + e8:half + 2 * e8], op=ALU.add)
                            e16 = e8 // 2
                            v.tensor_tensor(out=l4f[:, :], in0=hb[:, half + q:half + q + e16],
                                            in1=hb[:, half + q + e16:half + q + 2 * e16], op=ALU.add)
                            yv = yb[:, :].rearrange("p (a t) -> p a t", a=EH)[:, :, ts0:ts0 + TS]
                            i = v.tensor_tensor(
                                out=yv, in0=yv,
                                in1=l4f[:, :].rearrange("p (a t) -> p a t", a=EH),
                                op=ALU.add)
                            sem.inc("vv", i)
                        sub_done.append(sem.val("vv"))
                        if s == NSUB // 2 - 1:
                            emit_ffn_batch(ffn_jobs[:2])
                        elif s == NSUB - 1:
                            emit_ffn_batch(ffn_jobs[2:])

                    y_done.append(sub_done[-1])

                emit_wout(NCHUNK - 1)
                for tf_ in (14, 15):
                    emit_ffn_tile(1, tf_)
                    emit_ffn_fin()
            barrier()
        barrier()

        @block.sync
        def _(sync):
            sem.wait(sync, "sync", "sts", sem.val("sts"))

    return nc


def kernel(L0_attn_x, L0_attn_Wx, L0_attn_A_log, L0_attn_D, L0_attn_Wout,
           L0_ffn_w1, L0_ffn_b1, L0_ffn_w2, L0_ffn_b2):
    x = np.asarray(L0_attn_x, dtype=np.float32)
    Wx = np.asarray(L0_attn_Wx, dtype=np.float32)
    A_log = np.asarray(L0_attn_A_log, dtype=np.float32)
    D = np.asarray(L0_attn_D, dtype=np.float32)
    Wout = np.asarray(L0_attn_Wout, dtype=np.float32)
    w1 = np.asarray(L0_ffn_w1, dtype=np.float32)
    b1 = np.asarray(L0_ffn_b1, dtype=np.float32)
    w2 = np.asarray(L0_ffn_w2, dtype=np.float32)
    b2 = np.asarray(L0_ffn_b2, dtype=np.float32)

    wxt = np.ascontiguousarray(np.concatenate([Wx[32:], Wx[:32]], axis=0).T)
    wot = np.ascontiguousarray(Wout.T)
    w1c = np.ascontiguousarray(w1)
    w2c = np.ascontiguousarray(w2)
    alog = np.ascontiguousarray(A_log.reshape(1, 16))
    dvec = np.ascontiguousarray(D.reshape(8, 128).T)
    b1m = np.ascontiguousarray(b1.reshape(32, 128).T)
    b2m = np.ascontiguousarray(b2.reshape(8, 128).T)
    ohb = np.kron(np.eye(32, dtype=np.float32), np.ones((1, 128), np.float32))

    nc = build_program()
    in_maps = []
    for i in range(8):
        in_maps.append({
            "xin": np.ascontiguousarray(x[i].T),
            "wxt": wxt, "w1": w1c, "w2": w2c, "wot": wot,
            "alog": alog, "dvec": dvec, "b1m": b1m, "b2m": b2m, "ohb": ohb,
        })
    res = run_bass_kernel_spmd(nc, in_maps, core_ids=list(range(8)))
    out = np.stack([np.asarray(r["yout"], dtype=np.float32).T for r in res.results], axis=0)
    return np.ascontiguousarray(out)
